# version 33
# baseline (speedup 1.0000x reference)
import contextlib
import numpy as np
import ml_dtypes
import concourse.bass as bass
import concourse.mybir as mybir
from concourse.bass_utils import run_bass_kernel_spmd

F32 = mybir.dt.float32
BF16 = mybir.dt.bfloat16
AF = mybir.ActivationFunctionType
ALU = mybir.AluOpType
AX = mybir.AxisListType.X

EPS = 1e-5
SCALE = 0.0625


class Sched:
    ENGINES = ("tensor", "vector", "scalar", "sync", "gpsimd")
    SELF = {"vector": "V", "scalar": "S"}

    def __init__(self):
        self.q = {e: [] for e in self.ENGINES}
        self.cnt = {}

    def val(self, sem):
        return self.cnt.get(sem, 0)

    def em(self, eng, fn, waits=(), inc=None, inc_by=1):
        w = tuple((s, v) for (s, v) in waits if v > 0)
        if eng in self.SELF:
            assert inc in (None, self.SELF[eng])
            inc, inc_by = self.SELF[eng], 1
        if inc is not None:
            self.cnt[inc] = self.cnt.get(inc, 0) + inc_by
        self.q[eng].append((w, fn, inc, inc_by))

    def replay(self, eng_name, eng, sems):
        hw = {}
        for (waits, fn, inc, inc_by) in self.q[eng_name]:
            for s, v in waits:
                if hw.get(s, -1) < v:
                    eng.wait_ge(sems[s], v)
                    hw[s] = v
            inst = fn(eng)
            if inc is not None:
                inst.then_inc(sems[inc], inc_by)


def build_program(T=4096, use_preload=False):
    nc = bass.Bass("TRN2", target_bir_lowering=False)
    NCH = T // 128
    NSC = T // 256

    x_d = nc.dram_tensor("x", [T, 1024], F32, kind="ExternalInput").ap()
    w_d = {nm: nc.dram_tensor(nm, [1024, 1024], F32, kind="ExternalInput").ap()
           for nm in ("wq", "wk", "wv", "wo", "wrt", "wkt", "wvt")}
    wf_d = nc.dram_tensor("wf", [1024, 256], F32, kind="ExternalInput").ap()
    mur_d = nc.dram_tensor("mur", [128, 8], F32, kind="ExternalInput").ap()
    muk_d = nc.dram_tensor("muk", [128, 8], F32, kind="ExternalInput").ap()
    gnw_d = nc.dram_tensor("gnw", [128, 2], F32, kind="ExternalInput").ap()
    identb_d = nc.dram_tensor("identb", [128, 128], BF16, kind="ExternalInput").ap()
    identf_d = nc.dram_tensor("identf", [128, 128], F32, kind="ExternalInput").ap()
    mask4_d = nc.dram_tensor("mask4", [128, 512], BF16, kind="ExternalInput").ap()
    utri_d = nc.dram_tensor("utri", [128, 128], BF16, kind="ExternalInput").ap()
    sellast_d = nc.dram_tensor("sellast", [128, 128], F32, kind="ExternalInput").ap()
    out_d = nc.dram_tensor("out", [T, 1024], F32, kind="ExternalOutput").ap()

    es = contextlib.ExitStack()
    with es:
        sb = lambda name, shape, dt: es.enter_context(nc.sbuf_tensor(name, shape, dt))
        ps = lambda name, shape, dt=F32: es.enter_context(nc.psum_tensor(name, shape, dt))

        w_sb = {nm: sb(nm + "_sb", [128, 8192], BF16)
                for nm in ("wq", "wk", "wv", "wo", "wrt", "wkt", "wvt")}
        wf_sb = sb("wf_sb", [128, 2048], BF16)
        identb = sb("identb_sb", [128, 128], BF16)
        identf = sb("identf_sb", [128, 128], F32)
        mask4 = sb("mask4_sb", [128, 512], BF16)
        utri = sb("utri_sb", [128, 128], BF16)
        sellast = sb("sellast_sb", [128, 128], F32)
        mur = sb("mur_sb", [128, 8], F32)
        muk = sb("muk_sb", [128, 8], F32)
        gn = sb("gn_sb", [128, 2], F32)
        xtail = sb("xtail", [128, 8], BF16)

        xnT = sb("xnT", [128, 2048], BF16)
        kact = xnT
        q_sb = sb("q_sb", [128, 2048], BF16)
        k_sb = sb("k_sb", [128, 2048], BF16)
        v_sb = sb("v_sb", [128, 2048], BF16)
        Sp_sb = sb("Sp_sb", [128, 512], BF16)
        lam_sb = sb("lam_sb", [128, 512], F32)
        fstage = sb("fstage", [128, 2048], F32)
        xr_t = fstage.bitcast(BF16)
        stage2 = sb("stage2", [128, 2048], F32)
        t_r = stage2
        xnprevT = stage2.bitcast(BF16)
        xb = [sb("xb0", [128, 1024], F32), sb("xb1", [128, 1024], F32)]
        out_sb = [sb("ob0", [128, 1024], F32), sb("ob1", [128, 1024], F32)]

        t_e = sb("t_e", [128, 512], F32)
        t_l = sb("t_l", [128, 512], F32)
        l_hi = sb("l_hi", [128, 512], BF16)
        l_lo = sb("l_lo", [128, 512], BF16)
        t_linv = sb("t_linv", [128, 512], F32)
        krelu = sb("krelu", [128, 512], BF16)
        tmp512 = sb("tmp512", [128, 512], F32)
        kT_sb = sb("kT_sb", [128, 1024], BF16)
        Am_sb = sb("Am_sb", [128, 512], BF16)
        Bm_sb = sb("Bm_sb", [128, 512], BF16)
        Pt_sb = sb("Pt_sb", [128, 256], BF16)
        ST_sb = sb("ST_sb", [64, 512], BF16)
        PtT_sb = sb("PtT_sb", [64, 512], BF16)
        ssq = sb("ssq", [128, 4], F32)
        rstd4 = sb("rstd4", [128, 4], F32)
        lnscr = sb("lnscr", [128, 8], F32)
        c_eps = sb("c_eps", [128, 1], F32)
        c_one = sb("c_one", [128, 1], F32)
        o_bf = sb("o_bf", [128, 1024], BF16)
        sqs = krelu
        xn_bf = sb("xn_bf", [128, 2048], BF16)
        oT_sb = xn_bf
        spp = sb("spp", [128, 256], BF16)
        lbc_sb = sb("lbc_sb", [128, 256], F32)
        letc = sb("letc", [64, 4], F32)
        hk_sb = sb("hk_sb", [128, 512], F32)
        hk_bf = sb("hk_bf", [128, 512], BF16)
        hk_tmp = tmp512
        hv_sb = [sb("hv0", [64, 1024], F32), sb("hv1", [64, 1024], F32)]
        hv_bf = sb("hv_bf", [64, 1024], BF16)

        pA = ps("pA", [128, 512])
        pB = ps("pB", [128, 512])
        pC = ps("pC", [128, 512])
        pD = ps("pD", [128, 512])
        pE = ps("pE", [128, 512])
        pF = ps("pF", [128, 512])
        pT = ps("pT", [128, 1024], BF16)
        pT2 = ps("pT2", [128, 1024], BF16)

        sem_names = ("XIN", "XIN2", "WIN", "WIN2", "WC", "OUT0", "OUT1", "T", "V", "S", "G")
        sems = {n: es.enter_context(nc.semaphore(n)) for n in sem_names}
        blk = es.enter_context(nc.Block())

        sch = Sched()
        em, val = sch.em, sch.val

        gateV = {}
        gateS = {}

        def bank_gates(b):
            out = []
            if gateV.get(id(b), 0):
                out.append(("V", gateV[id(b)]))
            if gateS.get(id(b), 0):
                out.append(("S", gateS[id(b)]))
            return out

        def reader(b, sem):
            d = gateV if sem == "V" else gateS
            d[id(b)] = val(sem)

        def mm(out_ap, lhsT_ap, rhs_ap, start, stop, waits=(), inc=None):
            em("tensor",
               lambda e, o=out_ap, l=lhsT_ap, r=rhs_ap, st=start, sp=stop:
                   e.matmul(out=o, lhsT=l, rhs=r, start=st, stop=sp),
               waits=waits, inc=inc)

        def tr(out_ap, in_ap, ident, waits=(), inc=None):
            em("tensor",
               lambda e, o=out_ap, i=in_ap, d=ident: e.transpose(out=o, in_=i, identity=d),
               waits=waits, inc=inc)

        if use_preload:
            em("scalar", lambda e: e.preload_act_table(AF.Exp))

        def wdma(src_ap, dst_ap, queue, waits=()):
            sem = "WIN" if queue == "sync" else "WIN2"
            em(queue, lambda e, s=src_ap, d=dst_ap: e.dma_start(out=d, in_=s),
               waits=waits, inc=sem, inc_by=16)
            return (sem, val(sem))

        for csrc, cdst in [(identb_d, identb), (identf_d, identf), (mask4_d, mask4),
                           (utri_d, utri), (sellast_d, sellast), (mur_d, mur),
                           (muk_d, muk), (gnw_d, gn)]:
            em("sync", lambda e, s=csrc, d=cdst: e.dma_start(out=d[:], in_=s),
               inc="WC", inc_by=16)
        CONSTS = ("WC", val("WC"))
        em("vector", lambda e: e.memset(xtail[:], 0.0))
        em("vector", lambda e: e.memset(c_eps[:], EPS))
        em("vector", lambda e: e.memset(c_one[:], 1.0), inc="V")
        cbias_done = val("V")

        xin_val = {}

        def xdma(c, queue, waits=()):
            sem = "XIN" if queue == "sync" else "XIN2"
            em(queue, lambda e, c=c: e.dma_start(
                out=xb[c % 2][:], in_=x_d[c * 128:(c + 1) * 128, :]),
               waits=waits, inc=sem, inc_by=16)
            return (sem, val(sem))

        xin_val[0] = xdma(0, "sync")
        if NCH > 1:
            xin_val[1] = xdma(1, "sync")
            xin_val[0] = xin_val[1]

        WQ = []
        for nm in ("wq", "wk", "wv"):
            WQ += [(nm, qt) for qt in range(4)]
        WQ.append(("wf", 0))
        for nm in ("wo", "wrt", "wkt", "wvt"):
            WQ += [(nm, qt) for qt in range(4)]

        wdma_val = {}
        cast_val = {}

        def issue_wdma(n):
            nm, qt = WQ[n]
            stg = fstage if n % 2 == 0 else stage2
            waits = [("V", cast_val[n - 2])] if n - 2 >= 0 else []
            if nm == "wf":
                src = wf_d.rearrange("(i p) c -> p i c", p=128)
                dst = stg[:].rearrange("p (i c) -> p i c", i=8)
            else:
                src = w_d[nm].rearrange("(i p) c -> p i c", p=128)[:, 2 * qt:2 * qt + 2, :]
                dst = stg[:].rearrange("p (i c) -> p i c", i=2)
            wdma_val[n] = wdma(src, dst, "sync" if n % 2 == 0 else "gpsimd", waits=waits)

        issue_wdma(0)
        issue_wdma(1)

        cast_cursor = [0]
        wgate = {}

        def plan_cast(n):
            nm, qt = WQ[n]
            stg = fstage if n % 2 == 0 else stage2
            waits = [wdma_val[n]]
            if nm == "wf":
                em("vector", lambda e, s=stg: e.tensor_copy(out=wf_sb[:], in_=s[:]),
                   waits=waits, inc="V")
            elif nm == "wo":
                for iq in range(2):
                    i = 2 * qt + iq
                    em("vector",
                       lambda e, s=stg, iq=iq, i=i: e.tensor_scalar_mul(
                           out=w_sb["wo"][:, i * 1024:(i + 1) * 1024],
                           in0=s[:, iq * 1024:(iq + 1) * 1024],
                           scalar1=gn[:, (i % 2):(i % 2) + 1]),
                       waits=waits + [CONSTS], inc="V")
            elif nm == "wvt":
                em("vector",
                   lambda e, s=stg, qt=qt: e.tensor_scalar_mul(
                       out=w_sb["wvt"][:, qt * 2048:(qt + 1) * 2048], in0=s[:], scalar1=0.5),
                   waits=waits, inc="V")
            else:
                em("vector",
                   lambda e, s=stg, nm=nm, qt=qt: e.tensor_copy(
                       out=w_sb[nm][:, qt * 2048:(qt + 1) * 2048], in_=s[:]),
                   waits=waits, inc="V")
            cast_val[n] = val("V")
            if n + 2 < len(WQ):
                issue_wdma(n + 2)

        def need_weight(nm):
            if nm in wgate:
                return wgate[nm]
            last = max(i for i, (n2, _) in enumerate(WQ) if n2 == nm)
            while cast_cursor[0] <= last:
                plan_cast(cast_cursor[0])
                cast_cursor[0] += 1
            wgate[nm] = cast_val[last]
            return wgate[nm]

        def ptb(lane):
            return (pT, pT2)[lane]
        pC_free = [0]
        scT_free = [0]
        krelu_free = [0]
        obf_free = [0]
        xnbf_free = [0]
        kactT_free = [0]
        xnT_ready = [(0, 0)]
        hvbf_done = [0]
        hkbf_done = [0]
        out_n = [0]

        p0_state = {}

        def emit_P0_stats(sc):
            for j in (0, 1):
                c = 2 * sc + j
                xi = xin_val[c]
                em("vector", lambda e, c=c, j=j: e.reduce_sum(
                    out=lnscr[:, 0 + 4 * j:1 + 4 * j], in_=xb[c % 2][:], axis=AX),
                   waits=[xi])
                em("scalar", lambda e, c=c, j=j: e.activation(
                    out=xn_bf[:, j * 1024:(j + 1) * 1024], in_=xb[c % 2][:], func=AF.Square,
                    accum_out=lnscr[:, 2 + 4 * j:3 + 4 * j]),
                   waits=[xi, ("T", xnbf_free[0])] if j == 0 else [xi], inc="S")
            sq_done = val("S")
            for j in (0, 1):
                c = 2 * sc + j
                co = 4 * j
                em("vector", lambda e, co=co: e.tensor_scalar_mul(
                    out=lnscr[:, co + 1:co + 2], in0=lnscr[:, co:co + 1],
                    scalar1=-1.0 / 1024.0),
                   waits=[("S", sq_done), ("V", val("V"))] if j == 0 else ())
                em("vector", lambda e, co=co: e.tensor_scalar_mul(
                    out=lnscr[:, co + 2:co + 3], in0=lnscr[:, co + 2:co + 3],
                    scalar1=1.0 / 1024.0))
                em("vector", lambda e, co=co: e.tensor_tensor(
                    out=lnscr[:, co + 3:co + 4], in0=lnscr[:, co + 1:co + 2],
                    in1=lnscr[:, co + 1:co + 2], op=ALU.mult), waits=[("V", val("V"))])
                em("vector", lambda e, co=co: e.tensor_tensor(
                    out=lnscr[:, co + 2:co + 3], in0=lnscr[:, co + 2:co + 3],
                    in1=lnscr[:, co + 3:co + 4], op=ALU.subtract),
                   waits=[("V", val("V"))], inc="V")
            var_done = val("V")
            for j in (0, 1):
                co = 4 * j
                em("scalar", lambda e, co=co: e.activation(
                    out=lnscr[:, co + 2:co + 3], in_=lnscr[:, co + 2:co + 3],
                    func=AF.Ln, bias=c_eps[:, 0:1]),
                   waits=[("V", var_done), ("V", cbias_done)] if j == 0 else ())
                em("scalar", lambda e, co=co: e.activation(
                    out=lnscr[:, co + 2:co + 3], in_=lnscr[:, co + 2:co + 3],
                    func=AF.Exp, scale=-0.5), waits=[("S", val("S"))], inc="S")
            rstd_done = val("S")
            for j in (0, 1):
                c = 2 * sc + j
                xi = xin_val[c]
                co = 4 * j
                em("gpsimd", lambda e, c=c, j=j, co=co: e.tensor_scalar(
                    out=xn_bf[:, j * 1024:(j + 1) * 1024], in0=xb[c % 2][:],
                    scalar1=lnscr[:, co + 1:co + 2], scalar2=lnscr[:, co + 2:co + 3],
                    op0=ALU.add, op1=ALU.mult),
                   waits=([xi, ("S", rstd_done), ("V", var_done), ("T", xnbf_free[0])]
                          if j == 0 else [xi]), inc="G")
            p0_state["xn_done"] = ("G", val("G"))
            p0_state["scr_done"] = val("S")

        def emit_P0_trans(evac_eng="vector"):
            xn_done = p0_state["xn_done"]
            for j in (0, 1):
                for r in range(2):
                    bank = ptb(r)
                    for ii in range(4):
                        i = r * 4 + ii
                        w = ([xn_done, CONSTS] + bank_gates(bank)) if ii == 0 else ()
                        tr(bank[:, ii * 128:(ii + 1) * 128],
                           xn_bf[:, j * 1024 + i * 128: j * 1024 + i * 128 + 128],
                           identb[:], waits=w, inc="T" if ii == 3 else None)
                    tdone = val("T")
                    dst = xnT[:].rearrange("p (i jj t) -> p i jj t", i=8, jj=2)[:, r * 4:(r + 1) * 4, j, :]
                    if evac_eng == "vector":
                        em("vector", lambda e, d=dst, b=bank: e.tensor_copy(
                            out=d, in_=b[:, 0:512].rearrange("p (a b) -> p a b", a=4)),
                           waits=[("T", tdone), ("T", kactT_free[0]),
                                  ("S", p0_state["scr_done"])], inc="V")
                        reader(bank, "V")
                    else:
                        em("scalar", lambda e, d=dst, b=bank: e.copy(
                            out=d, in_=b[:, 0:512].rearrange("p (a b) -> p a b", a=4)),
                           waits=[("T", tdone), ("T", kactT_free[0]),
                                  ("S", p0_state["scr_done"])], inc="S")
                        reader(bank, "S")
            xnbf_free[0] = val("T")
            xnT_ready[0] = (val("V"), val("S"))

        def emit_qk(wname, dest, wg, xnT_doneV, xnT_doneS):
            for i in range(8):
                bank = (pA, pB)[(i // 2) % 2]
                region = bank[:, (i % 2) * 256:(i % 2) * 256 + 256]
                for d in range(8):
                    w = []
                    if d == 0:
                        w = [("V", wg), ("V", xnT_doneV), ("S", xnT_doneS)]
                        if i % 2 == 0:
                            w += bank_gates(bank)
                    mm(region, w_sb[wname][:, d * 1024 + i * 128: d * 1024 + i * 128 + 128],
                       xnT[:, d * 256:(d + 1) * 256],
                       start=(d == 0), stop=(d == 7), waits=w, inc="T" if d == 7 else None)
                if i % 2 == 1:
                    tdone = val("T")
                    em("scalar",
                       lambda e, b=bank, i=i, dst=dest: e.activation(
                           out=dst[:, (i - 1) * 256:(i + 1) * 256], in_=b[:, 0:512],
                           func=AF.Silu),
                       waits=[("T", tdone), ("T", scT_free[0])], inc="S")
                    reader(bank, "S")

        def emit_P(sc):
            xnT_doneV, xnT_doneS = xnT_ready[0]

            if use_preload:
                em("scalar", lambda e: e.preload_act_table(AF.Silu))
            emit_qk("wq", q_sb, need_weight("wq"), xnT_doneV, xnT_doneS)
            emit_qk("wk", k_sb, need_weight("wk"), xnT_doneV, xnT_doneS)
            qk_done = val("S")

            wg = need_weight("wv")
            for jj in range(2):
                for h2 in range(2):
                    bank = (pA, pB)[(jj * 2 + h2) % 2]
                    for d in range(8):
                        w = []
                        if d == 0:
                            w = [("V", wg), ("V", xnT_doneV), ("S", xnT_doneS)] + bank_gates(bank)
                        mm(bank[:, 0:512],
                           xnT[:, d * 256 + jj * 128: d * 256 + jj * 128 + 128],
                           w_sb["wv"][:, d * 1024 + h2 * 512: d * 1024 + h2 * 512 + 512],
                           start=(d == 0), stop=(d == 7), waits=w, inc="T" if d == 7 else None)
                    tdone = val("T")
                    em("scalar",
                       lambda e, b=bank, jj=jj, h2=h2: e.copy(
                           out=v_sb[:, jj * 1024 + h2 * 512: jj * 1024 + h2 * 512 + 512],
                           in_=b[:, 0:512]),
                       waits=[("T", tdone), ("T", scT_free[0])], inc="S")
                    reader(bank, "S")

            pre = {}
            for h in range(4):
                for ii in range(2):
                    g = 2 * h + ii
                    w = ([("S", qk_done)] + bank_gates(pD)) if (h == 0 and ii == 0) else ()
                    mm(pD[:, h * 128:(h + 1) * 128],
                       k_sb[:, g * 256: g * 256 + 128],
                       q_sb[:, g * 256: g * 256 + 128],
                       start=(ii == 0), stop=(ii == 1), waits=w,
                       inc="T" if (h == 3 and ii == 1) else None)
            pre["a_done"] = val("T")

            wg = need_weight("wf")
            for jj in range(2):
                for d in range(8):
                    w = []
                    if d == 0:
                        w = [("V", wg), ("V", xnT_doneV), ("S", xnT_doneS)]
                        if jj == 0:
                            w.append(("V", pC_free[0]))
                    mm(pC[:, jj * 256:(jj + 1) * 256],
                       xnT[:, d * 256 + jj * 128: d * 256 + jj * 128 + 128],
                       wf_sb[:, d * 256:(d + 1) * 256],
                       start=(d == 0), stop=(d == 7), waits=w, inc="T" if d == 7 else None)
            f_done = val("T")

            need_weight("wo"); need_weight("wrt"); need_weight("wkt")
            allw = need_weight("wvt")
            em("vector", lambda e: e.tensor_copy(out=xnprevT[:, 1:2048], in_=xnT[:, 0:2047]),
               waits=[("V", allw), CONSTS])
            em("vector", lambda e: e.tensor_copy(
                out=xnprevT[:, 0:2048].rearrange("p (i t) -> p i t", i=8)[:, :, 0],
                in_=xtail[:, 0:8]), waits=[("V", val("V"))])
            em("vector", lambda e: e.tensor_tensor(
                out=xr_t[:, 2048:4096], in0=xnT[:, 0:2048], in1=xnprevT[:, 0:2048],
                op=ALU.subtract), waits=[("V", val("V"))])
            for (mu, off) in ((mur, 0), (muk, 2048)):
                for i in range(8):
                    b = i * 256
                    em("vector", lambda e, b=b, i=i, mu=mu, off=off: e.scalar_tensor_tensor(
                        out=xr_t[:, off + b: off + b + 256], in0=xr_t[:, 2048 + b: 2048 + b + 256],
                        scalar=mu[:, i:i + 1], in1=xnprevT[:, b:b + 256],
                        op0=ALU.mult, op1=ALU.add),
                       waits=[("V", val("V"))] if i == 0 else ())
            em("vector", lambda e: e.tensor_copy(
                out=xtail[:, 0:8],
                in_=xnT[:].rearrange("p (i t) -> p i t", i=8)[:, :, 255]),
               waits=[("V", val("V"))], inc="V")
            mix_done = val("V")

            for jj in range(2):
                for h2 in range(2):
                    bank = (pA, pB)[(jj * 2 + h2) % 2]
                    for i in range(8):
                        w = []
                        if i == 0:
                            w = [("V", mix_done)] + bank_gates(bank)
                        mm(bank[:, 0:512],
                           xr_t[:, i * 256 + jj * 128: i * 256 + jj * 128 + 128],
                           w_sb["wrt"][:, i * 1024 + h2 * 512: i * 1024 + h2 * 512 + 512],
                           start=(i == 0), stop=(i == 7), waits=w, inc="T" if i == 7 else None)
                    tdone = val("T")
                    em("scalar", lambda e, b=bank, jj=jj, h2=h2: e.activation(
                        out=t_r[:, jj * 1024 + h2 * 512: jj * 1024 + h2 * 512 + 512],
                        in_=b[:, 0:512], func=AF.Tanh, scale=0.5),
                       waits=[("T", tdone)], inc="S")
                    reader(bank, "S")

            em("vector", lambda e: e.tensor_tensor(out=Am_sb[:], in0=pD[:, 0:512],
                                                   in1=mask4[:], op=ALU.mult),
               waits=[("T", pre["a_done"]), CONSTS, ("V", val("V"))], inc="V")
            reader(pD, "V")
            pre["am_done"] = val("V")

            if use_preload:
                em("scalar", lambda e: e.preload_act_table(AF.Exp))
            em("scalar", lambda e: e.activation(out=t_e[:], in_=pC[:, 0:512],
                                                func=AF.Exp, scale=-1.0),
               waits=[("T", f_done)])
            em("scalar", lambda e: e.activation(out=t_l[:], in_=t_e[:],
                                                func=AF.Ln, bias=c_one[:, 0:1]),
               waits=[("S", val("S"))], inc="S")
            l_all = val("S")
            em("vector", lambda e: e.tensor_copy(out=l_hi[:], in_=t_l[:]),
               waits=[("S", l_all), ("T", scT_free[0])])
            em("vector", lambda e: e.tensor_tensor(out=l_lo[:], in0=t_l[:], in1=l_hi[:],
                                                   op=ALU.subtract),
               waits=[("V", val("V"))], inc="V")
            hilo_all = val("V")

            for i2 in range(8):
                bank = (pA, pB)[(i2 // 2) % 2]
                region = bank[:, (i2 % 2) * 256:(i2 % 2) * 256 + 256]
                for d in range(8):
                    w = []
                    if d == 0:
                        w = [("V", mix_done)]
                        if i2 % 2 == 0:
                            w += bank_gates(bank)
                    mm(region,
                       w_sb["wkt"][:, d * 1024 + i2 * 128: d * 1024 + i2 * 128 + 128],
                       xr_t[:, 2048 + d * 256: 2048 + (d + 1) * 256],
                       start=(d == 0), stop=(d == 7), waits=w, inc="T" if d == 7 else None)
                if i2 % 2 == 1:
                    tdone = val("T")
                    em("scalar", lambda e, b=bank: e.activation(
                        out=krelu[:], in_=b[:, 0:512], func=AF.Relu),
                       waits=[("T", tdone), ("V", krelu_free[0])], inc="S")
                    sdone = val("S")
                    reader(bank, "S")
                    em("vector", lambda e, i2=i2: e.tensor_tensor(
                        out=kact[:, (i2 - 1) * 256:(i2 + 1) * 256], in0=krelu[:], in1=krelu[:],
                        op=ALU.mult),
                       waits=[("S", sdone), ("T", f_done)], inc="V")
                    krelu_free[0] = val("V")
            kact_done = val("V")

            for jj in range(2):
                mm(pC[:, jj * 256:(jj + 1) * 256], utri[:], l_hi[:, jj * 256:(jj + 1) * 256],
                   start=True, stop=False,
                   waits=[("V", hilo_all), ("S", l_all), CONSTS] if jj == 0 else ())
                mm(pC[:, jj * 256:(jj + 1) * 256], utri[:], l_lo[:, jj * 256:(jj + 1) * 256],
                   start=False, stop=True, inc="T" if jj == 1 else None)
            d_all = val("T")

            em("scalar", lambda e: e.activation(out=lam_sb[:], in_=pC[:, 0:512], func=AF.Exp),
               waits=[("T", d_all)])
            em("scalar", lambda e: e.activation(out=t_linv[:], in_=pC[:, 0:512],
                                                func=AF.Exp, scale=-1.0))
            em("scalar", lambda e: e.activation(out=t_e[:], in_=t_l[:], func=AF.Exp,
                                                scale=-0.125), waits=[("S", val("S"))], inc="S")
            s5_done = val("S")
            em("vector", lambda e: e.tensor_scalar(
                out=t_e[:], in0=t_e[:], scalar1=-1.0, scalar2=1.0,
                op0=ALU.mult, op1=ALU.add), waits=[("S", s5_done), ("V", val("V"))])
            em("vector", lambda e: e.tensor_tensor(out=Sp_sb[:], in0=t_e[:], in1=t_linv[:],
                                                   op=ALU.mult),
               waits=[("V", val("V"))], inc="V")
            Sp_done = val("V")

            for jj in range(2):
                c = 2 * sc + jj
                for h2 in range(2):
                    bank = (pA, pB)[(jj * 2 + h2) % 2]
                    for i2 in range(8):
                        w = []
                        if i2 == 0:
                            w = [("V", kact_done)] + bank_gates(bank)
                        mm(bank[:, 0:512],
                           kact[:, i2 * 256 + jj * 128: i2 * 256 + jj * 128 + 128],
                           w_sb["wvt"][:, i2 * 1024 + h2 * 512: i2 * 1024 + h2 * 512 + 512],
                           start=(i2 == 0), stop=(i2 == 7), waits=w, inc="T" if i2 == 7 else None)
                    tdone = val("T")
                    kactT_free[0] = tdone
                    em("vector", lambda e, b=bank, jj=jj, h2=h2: e.scalar_tensor_tensor(
                        out=tmp512[:],
                        in0=t_r[:, jj * 1024 + h2 * 512: jj * 1024 + h2 * 512 + 512],
                        scalar=1.0, in1=b[:, 0:512], op0=ALU.add, op1=ALU.mult),
                       waits=[("T", tdone), ("S", val("S")), ("V", val("V"))])
                    em("vector", lambda e, jj=jj, h2=h2, c=c: e.tensor_tensor(
                        out=fstage[:, jj * 1024 + h2 * 512: jj * 1024 + h2 * 512 + 512],
                        in0=tmp512[:], in1=xb[c % 2][:, h2 * 512:(h2 + 1) * 512],
                        op=ALU.add), waits=[("V", val("V"))], inc="V")
                    reader(bank, "V")
            F_done = val("V")
            pf = [2 * sc + j + 2 for j in (0, 1) if 2 * sc + j + 2 < NCH]
            for c in pf:
                xin_val[c] = xdma(c, "gpsimd", waits=[("V", F_done)])
            for c in pf:
                xin_val[c] = xin_val[pf[-1]]



            return Sp_done, s5_done, pre

        def emit_R_ha(sc, j, Sp_done, s5_done, pre):
            c = 2 * sc + j
            first = (c == 0)
            par = c % 2
            am_done = pre["am_done"]

            for h in range(4):
                w = []
                if h == 0:
                    w = [("V", am_done), ("V", Sp_done), ("S", s5_done),
                         ("V", pC_free[0])]
                mm(pC[:, j * 256 + h * 64: j * 256 + h * 64 + 64],
                   Am_sb[:, h * 128:(h + 1) * 128],
                   Sp_sb[:, j * 256 + h * 64: j * 256 + h * 64 + 64],
                   start=True, stop=first, waits=w,
                   inc=("T" if (first and h == 3) else None))
                if not first:
                    for ii in range(2):
                        wi = ([("V", hkbf_done[0])] if (h == 0 and ii == 0) else ())
                        mm(pC[:, j * 256 + h * 64: j * 256 + h * 64 + 64],
                           q_sb[:, (2 * h + ii) * 256 + j * 128:
                                (2 * h + ii) * 256 + j * 128 + 128],
                           hk_bf[:, (ii * 4 + h) * 64: (ii * 4 + h) * 64 + 64],
                           start=False, stop=(ii == 1), waits=wi,
                           inc=("T" if (h == 3 and ii == 1) else None))
            lg_done = val("T")

            base = 32 * par
            sm_lg = t_e[:, 0:256]
            sm_e = t_l[:, 0:256]
            nm4 = t_linv[:, base + 0:base + 4]
            nmS4 = t_linv[:, base + 4:base + 8]
            ssm4 = t_linv[:, base + 8:base + 12]
            rs4 = t_linv[:, base + 12:base + 16]
            v2_ = t_linv[:, base + 16:base + 20]
            v3_ = t_linv[:, base + 20:base + 24]
            sc4 = t_linv[:, base + 24:base + 28]
            em("vector", lambda e: e.tensor_tensor(
                out=sm_lg, in0=pC[:, j * 256:(j + 1) * 256],
                in1=lam_sb[:, j * 256:(j + 1) * 256], op=ALU.mult),
               waits=[("T", lg_done), ("V", val("V"))])
            reader(pC, "V")
            em("vector", lambda e: e.reduce_max(
                out=nm4[:, 0:1], in_=sm_lg, axis=AX, negate=True),
               waits=[("V", val("V"))])
            em("vector", lambda e: e.tensor_scalar_mul(out=nmS4[:, 0:1], in0=nm4[:, 0:1],
                                                       scalar1=SCALE),
               waits=[("V", val("V"))], inc="V")
            nms_done = val("V")

            bank = ptb(0)
            for h in range(4):
                w = bank_gates(bank) if h == 0 else ()
                tr(bank[0:64, h * 128:(h + 1) * 128],
                   Sp_sb[:, j * 256 + h * 64: j * 256 + h * 64 + 64], identb[:],
                   waits=w, inc="T" if h == 3 else None)
            tdone = val("T")
            em("vector", lambda e, b=bank: e.tensor_copy(out=ST_sb[0:64, 0:512],
                                                         in_=b[0:64, 0:512]),
               waits=[("T", tdone)], inc="V")
            reader(bank, "V")

            em("scalar", lambda e: e.activation(
                out=sm_e, in_=sm_lg, func=AF.Exp, scale=SCALE, bias=nmS4[:, 0:1]),
               waits=[("V", nms_done)], inc="S")
            sme_done = val("S")

            for half in range(2):
                bank = ptb(1)
                for ii in range(4):
                    i = half * 4 + ii
                    w = bank_gates(bank) if ii == 0 else ()
                    tr(bank[:, ii * 128:(ii + 1) * 128],
                       k_sb[:, i * 256 + j * 128: i * 256 + j * 128 + 128], identb[:],
                       waits=w, inc="T" if ii == 3 else None)
                tdone = val("T")
                em("scalar", lambda e, half=half, b=bank: e.copy(
                    out=kT_sb[:, half * 512:(half + 1) * 512], in_=b[:, 0:512]),
                   waits=[("T", tdone), ("T", scT_free[0])], inc="S")
                reader(bank, "S")
            kt_done = val("S")
            em("vector", lambda e: e.tensor_tensor(
                out=Pt_sb[:], in0=sm_e, in1=lam_sb[:, j * 256:(j + 1) * 256], op=ALU.mult),
               waits=[("S", sme_done), ("V", val("V"))], inc="V")
            pt_done = val("V")

            bank = ptb(0)
            for h in range(4):
                w = (bank_gates(bank) + [("V", pt_done)]) if h == 0 else ()
                tr(bank[0:64, h * 128:(h + 1) * 128], Pt_sb[:, h * 64:(h + 1) * 64], identb[:],
                   waits=w, inc="T" if h == 3 else None)
            tdone = val("T")
            em("vector", lambda e, b=bank: e.tensor_copy(out=PtT_sb[0:64, 0:512],
                                                         in_=b[0:64, 0:512]),
               waits=[("T", tdone)], inc="V")
            reader(bank, "V")
            ptT_done = val("V")

            for h in range(4):
                w = ([("V", ptT_done)] + bank_gates(pB)) if h == 0 else ()
                mm(pB[:, h * 128:(h + 1) * 128],
                   ST_sb[0:64, h * 128:(h + 1) * 128],
                   PtT_sb[0:64, h * 128:(h + 1) * 128],
                   start=True, stop=True, waits=w, inc="T" if h == 3 else None)
            b_done = val("T")
            em("vector", lambda e: e.tensor_tensor(out=Bm_sb[:], in0=pB[:, 0:512],
                                                   in1=mask4[:], op=ALU.mult),
               waits=[("T", b_done)], inc="V")
            reader(pB, "V")
            em("vector", lambda e: e.reduce_sum(
                out=ssm4, in_=sm_e.rearrange("p (h m) -> p h m", h=4), axis=AX),
               waits=[("V", val("V"))])
            em("vector", lambda e: e.reciprocal(out=rs4, in_=ssm4), waits=[("V", val("V"))],
               inc="V")
            return dict(c=c, first=first, par=par, j=j, bm_done=val("V"),
                        pt_done=pt_done, kt_done=kt_done,
                        rs4=rs4, v2_=v2_, v3_=v3_, sc4=sc4)

        def emit_R_hb(sc, j, st):
            c, first, par = st["c"], st["first"], st["par"]
            bm_done, pt_done, kt_done = st["bm_done"], st["pt_done"], st["kt_done"]

            for h in range(4):
                bank = pD if h < 2 else pE
                region = bank[:, (h % 2) * 256:(h % 2) * 256 + 256]
                w = ([("V", bm_done), ("V", hvbf_done[0])] + bank_gates(bank)) \
                    if h % 2 == 0 else ()
                last = (h % 2 == 1)
                mm(region, Bm_sb[:, h * 128:(h + 1) * 128],
                   v_sb[:, j * 1024 + h * 256: j * 1024 + h * 256 + 256],
                   start=True, stop=first, waits=w,
                   inc=("T" if (first and last) else None))
                if not first:
                    mm(region, PtT_sb[0:64, h * 128:(h + 1) * 128],
                       hv_bf[0:64, h * 256:(h + 1) * 256],
                       start=False, stop=True, inc=("T" if last else None))
                if h == 1:
                    st["oD_done"] = val("T")
            st["o_done"] = val("T")

            mm(pB[:, 0:256], sellast[:], lam_sb[:, j * 256:(j + 1) * 256],
               start=True, stop=True, waits=[("V", pt_done), CONSTS] + bank_gates(pB),
               inc="T")
            lbc_done = val("T")
            em("vector", lambda e: e.tensor_copy(out=lbc_sb[:], in_=pB[:, 0:256]),
               waits=[("T", lbc_done)])
            em("vector", lambda e: e.tensor_tensor(out=spp[:],
                                                   in0=Sp_sb[:, j * 256:(j + 1) * 256],
                                                   in1=lbc_sb[:], op=ALU.mult),
               waits=[("V", val("V"))], inc="V")
            spp_done = val("V")
            reader(pB, "V")
            letc_done = 0
            if not first:
                for h in range(4):
                    w = [("V", spp_done)] if h == 0 else ()
                    mm(pB[0:64, 256 + h: 256 + h + 1],
                       lbc_sb[:, h * 64:(h + 1) * 64], identf[:, 0:1],
                       start=True, stop=True, waits=w, inc="T" if h == 3 else None)
                tdone = val("T")
                em("vector", lambda e: e.tensor_copy(
                    out=letc[0:64, 0:4], in_=pB[0:64, 256:260]),
                   waits=[("T", tdone)], inc="V")
                reader(pB, "V")
                letc_done = val("V")
            st["spp_done"] = spp_done
            st["letc_done"] = letc_done

            for h in range(4):
                for ii in range(2):
                    w = ([("V", spp_done), ("S", kt_done)] + bank_gates(pF))                         if (h == 0 and ii == 0) else ()
                    mm(pF[:, (ii * 4 + h) * 64: (ii * 4 + h) * 64 + 64],
                       kT_sb[:, (2 * h + ii) * 128: (2 * h + ii) * 128 + 128],
                       spp[:, h * 64:(h + 1) * 64],
                       start=True, stop=True, waits=w,
                       inc="T" if (h == 3 and ii == 1) else None)
            hk_done = val("T")
            if first:
                em("vector", lambda e: e.tensor_copy(out=hk_sb[:], in_=pF[:, 0:512]),
                   waits=[("T", hk_done), ("V", val("V"))])
            else:
                for ii in range(2):
                    em("vector", lambda e, ii=ii: e.tensor_tensor(
                        out=hk_tmp[:, ii * 256:(ii + 1) * 256],
                        in0=hk_sb[:, ii * 256:(ii + 1) * 256], in1=lbc_sb[:], op=ALU.mult),
                       waits=[("T", hk_done), ("V", val("V"))] if ii == 0 else ())
                em("vector", lambda e: e.tensor_tensor(out=hk_sb[:], in0=hk_tmp[:],
                                                       in1=pF[:, 0:512], op=ALU.add),
                   waits=[("V", val("V"))])
            em("vector", lambda e: e.tensor_copy(out=hk_bf[:], in_=hk_sb[:]),
               waits=[("V", val("V"))], inc="V")
            hkbf_done[0] = val("V")
            reader(pF, "V")

            return st

        def emit_R_tail_a(sc, j, st):
            c, first, par = st["c"], st["first"], st["par"]
            oD_done, o_done = st["oD_done"], st["o_done"]
            spp_done, letc_done = st["spp_done"], st["letc_done"]
            rs4, v2_, v3_, sc4 = st["rs4"], st["v2_"], st["v3_"], st["sc4"]

            em("scalar", lambda e: e.copy(out=o_bf[:, 0:512], in_=pD[:, 0:512]),
               waits=[("T", oD_done), ("T", obf_free[0])], inc="S")
            reader(pD, "S")
            em("scalar", lambda e: e.copy(out=o_bf[:, 512:1024], in_=pE[:, 0:512]),
               waits=[("T", o_done)], inc="S")
            reader(pE, "S")
            ocp_done = val("S")
            em("vector", lambda e: e.tensor_tensor(out=v2_, in0=rs4, in1=rs4, op=ALU.mult),
               waits=[("V", val("V"))])

            for h in range(4):
                bank = pD if h < 2 else pE
                region = bank[0:64, (h % 2) * 256:(h % 2) * 256 + 256]
                w = ([("V", spp_done)] + bank_gates(bank)) if h % 2 == 0 else ()
                mm(region, spp[:, h * 64:(h + 1) * 64],
                   v_sb[:, j * 1024 + h * 256: j * 1024 + h * 256 + 256],
                   start=True, stop=True, waits=w, inc="T" if h == 3 else None)
            hv_done = val("T")
            hv_new, hv_old = hv_sb[par], hv_sb[1 - par]
            for h in range(4):
                bank = pD if h < 2 else pE
                region = bank[0:64, (h % 2) * 256:(h % 2) * 256 + 256]
                if first:
                    em("vector", lambda e, r=region, h=h: e.tensor_copy(
                        out=hv_new[0:64, h * 256:(h + 1) * 256], in_=r),
                       waits=[("T", hv_done), ("V", val("V"))] if h == 0 else ())
                else:
                    w = [("T", hv_done), ("V", letc_done), ("V", val("V"))] if h == 0 else ()
                    em("vector", lambda e, r=region, h=h: e.scalar_tensor_tensor(
                        out=hv_new[0:64, h * 256:(h + 1) * 256],
                        in0=hv_old[0:64, h * 256:(h + 1) * 256],
                        scalar=letc[0:64, h:h + 1], in1=r, op0=ALU.mult, op1=ALU.add),
                       waits=w)
            em("vector", lambda e: e.tensor_copy(out=hv_bf[0:64, :], in_=hv_new[0:64, :]),
               waits=[("V", val("V"))], inc="V")
            hvbf_done[0] = val("V")
            reader(pD, "V")
            reader(pE, "V")

            sq_hist = {}
            for h in range(4):
                w = [("S", ocp_done)] if h == 0 else (
                    [("S", sq_hist[h - 2])] if h >= 2 else ())
                em("scalar", lambda e, h=h: e.activation(
                    out=sqs[:, (h % 2) * 256:(h % 2) * 256 + 256],
                    in_=o_bf[:, h * 256:(h + 1) * 256], func=AF.Square,
                    accum_out=ssq[:, h:h + 1]),
                   waits=w, inc="S" if h == 3 else None)
                sq_hist[h] = val("S")
            sq_done = val("S")
            em("vector", lambda e: e.tensor_tensor(out=v3_, in0=ssq[:], in1=v2_, op=ALU.mult),
               waits=[("S", sq_done), ("V", val("V"))], inc="V")
            v3_done = val("V")
            em("scalar", lambda e: e.activation(out=rstd4[:], in_=v3_, func=AF.Ln,
                                                scale=1.0 / 256.0, bias=c_eps[:, 0:1]),
               waits=[("V", v3_done)])
            em("scalar", lambda e: e.activation(out=rstd4[:], in_=rstd4[:], func=AF.Exp,
                                                scale=-0.5), waits=[("S", val("S"))], inc="S")
            rstd_done = val("S")
            em("vector", lambda e: e.tensor_tensor(out=sc4, in0=rstd4[:], in1=rs4,
                                                   op=ALU.mult),
               waits=[("S", rstd_done), ("V", val("V"))], inc="V")
            obf_pair = []
            for h in range(4):
                w = [("V", val("V"))] if h == 0 else ()
                em("vector", lambda e, h=h: e.tensor_scalar_mul(
                    out=o_bf[:, h * 256:(h + 1) * 256],
                    in0=o_bf[:, h * 256:(h + 1) * 256], scalar1=sc4[:, h:h + 1]),
                   waits=w, inc="V" if h % 2 == 1 else None)
                if h % 2 == 1:
                    obf_pair.append(val("V"))
            st["obf_pair"] = obf_pair
            return st

        def emit_R_tail_b(sc, j, st):
            c, first, par = st["c"], st["first"], st["par"]
            obf_pair = st["obf_pair"]

            for r in range(2):
                bank = ptb(r)
                for ii in range(4):
                    i = r * 4 + ii
                    w = (bank_gates(bank) + [("V", obf_pair[r])]) if ii == 0 else ()
                    tr(bank[:, ii * 128:(ii + 1) * 128], o_bf[:, i * 128:(i + 1) * 128],
                       identb[:], waits=w, inc="T" if ii == 3 else None)
                tdone = val("T")
                em("scalar", lambda e, r=r, b=bank, j=j: e.copy(
                    out=oT_sb[:, j * 1024 + r * 512: j * 1024 + (r + 1) * 512],
                    in_=b[:, 0:512]),
                   waits=[("T", tdone)], inc="S")
                reader(bank, "S")
            oT_done = val("S")
            obf_free[0] = val("T")

            for h2 in range(2):
                bank = (pC, pF)[h2]
                for i in range(8):
                    w = ([("S", oT_done)] + bank_gates(bank)) if i == 0 else ()
                    mm(bank[:, 0:512], oT_sb[:, j * 1024 + i * 128: j * 1024 + (i + 1) * 128],
                       w_sb["wo"][:, i * 1024 + h2 * 512: i * 1024 + h2 * 512 + 512],
                       start=(i == 0), stop=(i == 7), waits=w, inc="T" if i == 7 else None)
                tdone = val("T")
                ow = [("T", tdone)]
                if out_n[0] >= 2:
                    ow.append(("OUT%d" % (out_n[0] % 2), 16 * ((out_n[0] - 2) // 2 + 1)))
                em("vector", lambda e, b=bank, h2=h2, j=j, par=par: e.tensor_tensor(
                    out=out_sb[par][:, h2 * 512:(h2 + 1) * 512],
                    in0=b[:, 0:512],
                    in1=fstage[:, j * 1024 + h2 * 512: j * 1024 + h2 * 512 + 512],
                    op=ALU.add), waits=ow + [("V", val("V"))], inc="V")
                reader(bank, "V")
            comb_done = val("V")
            pC_free[0] = comb_done
            xnbf_free[0] = val("T")
            out_n[0] += 1
            em("gpsimd", lambda e, c=c, par=par: e.dma_start(
                out=out_d[c * 128:(c + 1) * 128, :], in_=out_sb[par][:]),
               waits=[("V", comb_done)], inc="OUT%d" % (c % 2), inc_by=16)

        def emit_A(j):
            pre = {}
            for h in range(4):
                for ii in range(2):
                    g = 2 * h + ii
                    w = bank_gates(pA) if (h == 0 and ii == 0) else ()
                    mm(pA[:, h * 128:(h + 1) * 128],
                       k_sb[:, g * 256 + j * 128: g * 256 + j * 128 + 128],
                       q_sb[:, g * 256 + j * 128: g * 256 + j * 128 + 128],
                       start=(ii == 0), stop=(ii == 1), waits=w,
                       inc="T" if (h == 3 and ii == 1) else None)
            pre["a_done"] = val("T")
            em("vector", lambda e: e.tensor_tensor(out=Am_sb[:], in0=pA[:, 0:512],
                                                   in1=mask4[:], op=ALU.mult),
               waits=[("T", pre["a_done"]), CONSTS], inc="V")
            reader(pA, "V")
            pre["am_done"] = val("V")
            return pre

        emit_P0_stats(0)
        emit_P0_trans("vector")
        for sc in range(NSC):
            Sp_done, s5_done, pre = emit_P(sc)
            st0 = emit_R_ha(sc, 0, Sp_done, s5_done, pre)
            pre1 = emit_A(1)
            st0 = emit_R_hb(sc, 0, st0)
            st1 = emit_R_ha(sc, 1, Sp_done, s5_done, pre1)
            st0 = emit_R_tail_a(sc, 0, st0)
            st1 = emit_R_hb(sc, 1, st1)
            emit_R_tail_b(sc, 0, st0)
            if sc + 1 < NSC:
                emit_P0_stats(sc + 1)
            st1 = emit_R_tail_a(sc, 1, st1)
            if sc + 1 < NSC:
                emit_P0_trans("vector")
            emit_R_tail_b(sc, 1, st1)
            scT_free[0] = val("T")

        em("gpsimd", lambda e: e.nop(),
           waits=[("OUT0", 16 * ((NCH + 1) // 2)), ("OUT1", 16 * (NCH // 2))])

        @blk.sync
        def _(e):
            sch.replay("sync", e, sems)

        @blk.gpsimd
        def _(e):
            sch.replay("gpsimd", e, sems)

        @blk.tensor
        def _(e):
            sch.replay("tensor", e, sems)

        @blk.vector
        def _(e):
            sch.replay("vector", e, sems)

        @blk.scalar
        def _(e):
            sch.replay("scalar", e, sems)

    return nc


def _prep_inputs(inputs):
    f32 = np.float32
    x = np.ascontiguousarray(np.asarray(inputs["L0_attn_x"], dtype=f32))
    com = {
        "wq": np.ascontiguousarray(np.asarray(inputs["L0_attn_Wq"], f32)),
        "wk": np.ascontiguousarray(np.asarray(inputs["L0_attn_Wk"], f32)),
        "wv": np.ascontiguousarray(np.asarray(inputs["L0_attn_Wv"], f32)),
        "wf": np.ascontiguousarray(np.asarray(inputs["L0_attn_Wf"], f32)),
        "wo": np.ascontiguousarray(np.asarray(inputs["L0_attn_Wo"], f32)),
        "wrt": np.ascontiguousarray(np.asarray(inputs["L0_ffn_W_r"], f32).T),
        "wkt": np.ascontiguousarray(np.asarray(inputs["L0_ffn_W_k"], f32).T),
        "wvt": np.ascontiguousarray(np.asarray(inputs["L0_ffn_W_v"], f32).T),
        "mur": np.ascontiguousarray(np.asarray(inputs["L0_ffn_mu_r"], f32).reshape(8, 128).T),
        "muk": np.ascontiguousarray(np.asarray(inputs["L0_ffn_mu_k"], f32).reshape(8, 128).T),
        "gnw": np.ascontiguousarray(np.asarray(inputs["L0_attn_g_norm_w"], f32).reshape(2, 128).T),
        "identb": np.eye(128, dtype=ml_dtypes.bfloat16),
        "identf": np.eye(128, dtype=np.float32),
        "mask4": np.ascontiguousarray(
            np.tile(np.triu(np.ones((128, 128), np.float32)), (1, 4)).astype(ml_dtypes.bfloat16)),
        "utri": np.ascontiguousarray(
            (np.triu(np.ones((128, 128), np.float32)) * -0.125).astype(ml_dtypes.bfloat16)),
        "sellast": np.ascontiguousarray(
            ((np.arange(128)[:, None] == 127) * np.ones((1, 128))).astype(np.float32)),
    }
    return x, com


def kernel(**inputs):
    x, com = _prep_inputs(inputs)
    nc = build_program(T=4096)
    in_maps = []
    for b in range(8):
        m = {"x": np.ascontiguousarray(x[b])}
        m.update(com)
        in_maps.append(m)
    res = run_bass_kernel_spmd(nc, in_maps, core_ids=list(range(8)))
    out = np.stack([np.asarray(r["out"]) for r in res.results], axis=0)
    return out.astype(np.float32)


# version 34
# speedup vs baseline: 1.0192x; 1.0192x over previous
import contextlib
import numpy as np
import ml_dtypes
import concourse.bass as bass
import concourse.mybir as mybir
from concourse.bass_utils import run_bass_kernel_spmd

F32 = mybir.dt.float32
BF16 = mybir.dt.bfloat16
AF = mybir.ActivationFunctionType
ALU = mybir.AluOpType
AX = mybir.AxisListType.X

EPS = 1e-5
SCALE = 0.0625


class Sched:
    ENGINES = ("tensor", "vector", "scalar", "sync", "gpsimd")
    SELF = {"vector": "V", "scalar": "S"}

    def __init__(self):
        self.q = {e: [] for e in self.ENGINES}
        self.cnt = {}

    def val(self, sem):
        return self.cnt.get(sem, 0)

    def em(self, eng, fn, waits=(), inc=None, inc_by=1):
        w = tuple((s, v) for (s, v) in waits if v > 0)
        if eng in self.SELF:
            assert inc in (None, self.SELF[eng])
            inc, inc_by = self.SELF[eng], 1
        if inc is not None:
            self.cnt[inc] = self.cnt.get(inc, 0) + inc_by
        self.q[eng].append((w, fn, inc, inc_by))

    def replay(self, eng_name, eng, sems):
        hw = {}
        for (waits, fn, inc, inc_by) in self.q[eng_name]:
            for s, v in waits:
                if hw.get(s, -1) < v:
                    eng.wait_ge(sems[s], v)
                    hw[s] = v
            inst = fn(eng)
            if inc is not None:
                inst.then_inc(sems[inc], inc_by)


def build_program(T=4096, use_preload=False):
    nc = bass.Bass("TRN2", target_bir_lowering=False)
    NCH = T // 128
    NSC = T // 256

    x_d = nc.dram_tensor("x", [T, 1024], F32, kind="ExternalInput").ap()
    w_d = {nm: nc.dram_tensor(nm, [1024, 1024], F32, kind="ExternalInput").ap()
           for nm in ("wq", "wk", "wv", "wo", "wrt", "wkt", "wvt")}
    wf_d = nc.dram_tensor("wf", [1024, 256], F32, kind="ExternalInput").ap()
    mur_d = nc.dram_tensor("mur", [128, 8], F32, kind="ExternalInput").ap()
    muk_d = nc.dram_tensor("muk", [128, 8], F32, kind="ExternalInput").ap()
    gnw_d = nc.dram_tensor("gnw", [128, 2], F32, kind="ExternalInput").ap()
    identb_d = nc.dram_tensor("identb", [128, 128], BF16, kind="ExternalInput").ap()
    identf_d = nc.dram_tensor("identf", [128, 128], F32, kind="ExternalInput").ap()
    mask4_d = nc.dram_tensor("mask4", [128, 512], BF16, kind="ExternalInput").ap()
    utri_d = nc.dram_tensor("utri", [128, 128], BF16, kind="ExternalInput").ap()
    sellast_d = nc.dram_tensor("sellast", [128, 128], F32, kind="ExternalInput").ap()
    out_d = nc.dram_tensor("out", [T, 1024], F32, kind="ExternalOutput").ap()

    es = contextlib.ExitStack()
    with es:
        sb = lambda name, shape, dt: es.enter_context(nc.sbuf_tensor(name, shape, dt))
        ps = lambda name, shape, dt=F32: es.enter_context(nc.psum_tensor(name, shape, dt))

        w_sb = {nm: sb(nm + "_sb", [128, 8192], BF16)
                for nm in ("wq", "wk", "wv", "wo", "wrt", "wkt", "wvt")}
        wf_sb = sb("wf_sb", [128, 2048], BF16)
        identb = sb("identb_sb", [128, 128], BF16)
        identf = sb("identf_sb", [128, 128], F32)
        mask4 = sb("mask4_sb", [128, 512], BF16)
        utri = sb("utri_sb", [128, 128], BF16)
        sellast = sb("sellast_sb", [128, 128], F32)
        mur = sb("mur_sb", [128, 8], F32)
        muk = sb("muk_sb", [128, 8], F32)
        gn = sb("gn_sb", [128, 2], F32)
        xtail = sb("xtail", [128, 8], BF16)

        xnT = sb("xnT", [128, 2048], BF16)
        kact = xnT
        q_sb = sb("q_sb", [128, 2048], BF16)
        k_sb = sb("k_sb", [128, 2048], BF16)
        v_sb = sb("v_sb", [128, 2048], BF16)
        Sp_sb = sb("Sp_sb", [128, 512], BF16)
        lam_sb = sb("lam_sb", [128, 512], F32)
        fstage = sb("fstage", [128, 2048], F32)
        xr_t = fstage.bitcast(BF16)
        stage2 = sb("stage2", [128, 2048], F32)
        t_r = stage2
        xnprevT = stage2.bitcast(BF16)
        xb = [sb("xb0", [128, 1024], F32), sb("xb1", [128, 1024], F32)]
        out_sb = [sb("ob0", [128, 1024], F32), sb("ob1", [128, 1024], F32)]

        t_e = sb("t_e", [128, 512], F32)
        t_l = sb("t_l", [128, 512], F32)
        l_hi = sb("l_hi", [128, 512], BF16)
        l_lo = sb("l_lo", [128, 512], BF16)
        t_linv = sb("t_linv", [128, 512], F32)
        krelu = sb("krelu", [128, 512], BF16)
        tmp512 = sb("tmp512", [128, 512], F32)
        kT_sb = sb("kT_sb", [128, 1024], BF16)
        Am_sb = sb("Am_sb", [128, 512], BF16)
        Bm_sb = sb("Bm_sb", [128, 512], BF16)
        Pt_sb = sb("Pt_sb", [128, 256], BF16)
        ST_sb = sb("ST_sb", [64, 512], BF16)
        PtT_sb = sb("PtT_sb", [64, 512], BF16)
        ssq = sb("ssq", [128, 4], F32)
        rstd4 = sb("rstd4", [128, 4], F32)
        lnscr = sb("lnscr", [128, 8], F32)
        c_eps = sb("c_eps", [128, 1], F32)
        c_one = sb("c_one", [128, 1], F32)
        o_bf = sb("o_bf", [128, 1024], BF16)
        sqs = krelu
        xn_bf = sb("xn_bf", [128, 2048], BF16)
        oT_sb = xn_bf
        spp = sb("spp", [128, 256], BF16)
        lbc_sb = sb("lbc_sb", [128, 256], F32)
        letc = sb("letc", [64, 4], F32)
        hk_sb = sb("hk_sb", [128, 512], F32)
        hk_bf = sb("hk_bf", [128, 512], BF16)
        hk_tmp = tmp512
        hv_sb = [sb("hv0", [64, 1024], F32), sb("hv1", [64, 1024], F32)]
        hv_bf = sb("hv_bf", [64, 1024], BF16)

        pA = ps("pA", [128, 512])
        pB = ps("pB", [128, 512])
        pC = ps("pC", [128, 512])
        pD = ps("pD", [128, 512])
        pE = ps("pE", [128, 512])
        pF = ps("pF", [128, 512])
        pT = ps("pT", [128, 1024], BF16)
        pT2 = ps("pT2", [128, 1024], BF16)

        sem_names = ("XIN", "XIN2", "WIN", "WIN2", "WC", "OUT0", "OUT1", "T", "V", "S", "G")
        sems = {n: es.enter_context(nc.semaphore(n)) for n in sem_names}
        blk = es.enter_context(nc.Block())

        sch = Sched()
        em, val = sch.em, sch.val

        gateV = {}
        gateS = {}

        def bank_gates(b):
            out = []
            if gateV.get(id(b), 0):
                out.append(("V", gateV[id(b)]))
            if gateS.get(id(b), 0):
                out.append(("S", gateS[id(b)]))
            return out

        def reader(b, sem):
            d = gateV if sem == "V" else gateS
            d[id(b)] = val(sem)

        def mm(out_ap, lhsT_ap, rhs_ap, start, stop, waits=(), inc=None):
            em("tensor",
               lambda e, o=out_ap, l=lhsT_ap, r=rhs_ap, st=start, sp=stop:
                   e.matmul(out=o, lhsT=l, rhs=r, start=st, stop=sp),
               waits=waits, inc=inc)

        def tr(out_ap, in_ap, ident, waits=(), inc=None):
            em("tensor",
               lambda e, o=out_ap, i=in_ap, d=ident: e.transpose(out=o, in_=i, identity=d),
               waits=waits, inc=inc)

        if use_preload:
            em("scalar", lambda e: e.preload_act_table(AF.Exp))

        def wdma(src_ap, dst_ap, queue, waits=()):
            sem = "WIN" if queue == "sync" else "WIN2"
            em(queue, lambda e, s=src_ap, d=dst_ap: e.dma_start(out=d, in_=s),
               waits=waits, inc=sem, inc_by=16)
            return (sem, val(sem))

        for csrc, cdst in [(identb_d, identb), (identf_d, identf), (mask4_d, mask4),
                           (utri_d, utri), (sellast_d, sellast), (mur_d, mur),
                           (muk_d, muk), (gnw_d, gn)]:
            em("sync", lambda e, s=csrc, d=cdst: e.dma_start(out=d[:], in_=s),
               inc="WC", inc_by=16)
        CONSTS = ("WC", val("WC"))
        em("vector", lambda e: e.memset(xtail[:], 0.0))
        em("vector", lambda e: e.memset(c_eps[:], EPS))
        em("vector", lambda e: e.memset(c_one[:], 1.0), inc="V")
        cbias_done = val("V")

        xin_val = {}

        def xdma(c, queue, waits=()):
            sem = "XIN" if queue == "sync" else "XIN2"
            em(queue, lambda e, c=c: e.dma_start(
                out=xb[c % 2][:], in_=x_d[c * 128:(c + 1) * 128, :]),
               waits=waits, inc=sem, inc_by=16)
            return (sem, val(sem))

        xin_val[0] = xdma(0, "sync")
        if NCH > 1:
            xin_val[1] = xdma(1, "sync")
            xin_val[0] = xin_val[1]

        WQ = []
        for nm in ("wq", "wk", "wv"):
            WQ += [(nm, qt) for qt in range(4)]
        WQ.append(("wf", 0))
        for nm in ("wo", "wrt", "wkt", "wvt"):
            WQ += [(nm, qt) for qt in range(4)]

        wdma_val = {}
        cast_val = {}

        def issue_wdma(n):
            nm, qt = WQ[n]
            stg = fstage if n % 2 == 0 else stage2
            waits = [("V", cast_val[n - 2])] if n - 2 >= 0 else []
            if nm == "wf":
                src = wf_d.rearrange("(i p) c -> p i c", p=128)
                dst = stg[:].rearrange("p (i c) -> p i c", i=8)
            else:
                src = w_d[nm].rearrange("(i p) c -> p i c", p=128)[:, 2 * qt:2 * qt + 2, :]
                dst = stg[:].rearrange("p (i c) -> p i c", i=2)
            wdma_val[n] = wdma(src, dst, "sync" if n % 2 == 0 else "gpsimd", waits=waits)

        issue_wdma(0)
        issue_wdma(1)

        cast_cursor = [0]
        wgate = {}

        def plan_cast(n):
            nm, qt = WQ[n]
            stg = fstage if n % 2 == 0 else stage2
            waits = [wdma_val[n]]
            if nm == "wf":
                em("vector", lambda e, s=stg: e.tensor_copy(out=wf_sb[:], in_=s[:]),
                   waits=waits, inc="V")
            elif nm == "wo":
                for iq in range(2):
                    i = 2 * qt + iq
                    em("vector",
                       lambda e, s=stg, iq=iq, i=i: e.tensor_scalar_mul(
                           out=w_sb["wo"][:, i * 1024:(i + 1) * 1024],
                           in0=s[:, iq * 1024:(iq + 1) * 1024],
                           scalar1=gn[:, (i % 2):(i % 2) + 1]),
                       waits=waits + [CONSTS], inc="V")
            elif nm == "wvt":
                em("vector",
                   lambda e, s=stg, qt=qt: e.tensor_scalar_mul(
                       out=w_sb["wvt"][:, qt * 2048:(qt + 1) * 2048], in0=s[:], scalar1=0.5),
                   waits=waits, inc="V")
            else:
                em("vector",
                   lambda e, s=stg, nm=nm, qt=qt: e.tensor_copy(
                       out=w_sb[nm][:, qt * 2048:(qt + 1) * 2048], in_=s[:]),
                   waits=waits, inc="V")
            cast_val[n] = val("V")
            if n + 2 < len(WQ):
                issue_wdma(n + 2)

        def need_weight(nm):
            if nm in wgate:
                return wgate[nm]
            last = max(i for i, (n2, _) in enumerate(WQ) if n2 == nm)
            while cast_cursor[0] <= last:
                plan_cast(cast_cursor[0])
                cast_cursor[0] += 1
            wgate[nm] = cast_val[last]
            return wgate[nm]

        def ptb(lane):
            return (pT, pT2)[lane]
        pC_free = [0]
        scT_free = [0]
        krelu_free = [0]
        obf_free = [0, 0]
        v3_prev = [0]
        xnbf_free = [0]
        kactT_free = [0]
        xnT_ready = [(0, 0)]
        hvbf_done = [0]
        hkbf_done = [0]
        out_n = [0]

        p0_state = {}

        def emit_P0_stats(sc):
            for j in (0, 1):
                c = 2 * sc + j
                xi = xin_val[c]
                em("vector", lambda e, c=c, j=j: e.reduce_sum(
                    out=lnscr[:, 0 + 4 * j:1 + 4 * j], in_=xb[c % 2][:], axis=AX),
                   waits=[xi])
                em("scalar", lambda e, c=c, j=j: e.activation(
                    out=xn_bf[:, j * 1024:(j + 1) * 1024], in_=xb[c % 2][:], func=AF.Square,
                    accum_out=lnscr[:, 2 + 4 * j:3 + 4 * j]),
                   waits=[xi, ("T", xnbf_free[0])] if j == 0 else [xi], inc="S")
            sq_done = val("S")
            for j in (0, 1):
                c = 2 * sc + j
                co = 4 * j
                em("vector", lambda e, co=co: e.tensor_scalar_mul(
                    out=lnscr[:, co + 1:co + 2], in0=lnscr[:, co:co + 1],
                    scalar1=-1.0 / 1024.0),
                   waits=[("S", sq_done), ("V", val("V"))] if j == 0 else ())
                em("vector", lambda e, co=co: e.tensor_scalar_mul(
                    out=lnscr[:, co + 2:co + 3], in0=lnscr[:, co + 2:co + 3],
                    scalar1=1.0 / 1024.0))
                em("vector", lambda e, co=co: e.tensor_tensor(
                    out=lnscr[:, co + 3:co + 4], in0=lnscr[:, co + 1:co + 2],
                    in1=lnscr[:, co + 1:co + 2], op=ALU.mult), waits=[("V", val("V"))])
                em("vector", lambda e, co=co: e.tensor_tensor(
                    out=lnscr[:, co + 2:co + 3], in0=lnscr[:, co + 2:co + 3],
                    in1=lnscr[:, co + 3:co + 4], op=ALU.subtract),
                   waits=[("V", val("V"))], inc="V")
            var_done = val("V")
            for j in (0, 1):
                co = 4 * j
                em("scalar", lambda e, co=co: e.activation(
                    out=lnscr[:, co + 2:co + 3], in_=lnscr[:, co + 2:co + 3],
                    func=AF.Ln, bias=c_eps[:, 0:1]),
                   waits=[("V", var_done), ("V", cbias_done)] if j == 0 else ())
                em("scalar", lambda e, co=co: e.activation(
                    out=lnscr[:, co + 2:co + 3], in_=lnscr[:, co + 2:co + 3],
                    func=AF.Exp, scale=-0.5), waits=[("S", val("S"))], inc="S")
            rstd_done = val("S")
            for j in (0, 1):
                c = 2 * sc + j
                xi = xin_val[c]
                co = 4 * j
                em("gpsimd", lambda e, c=c, j=j, co=co: e.tensor_scalar(
                    out=xn_bf[:, j * 1024:(j + 1) * 1024], in0=xb[c % 2][:],
                    scalar1=lnscr[:, co + 1:co + 2], scalar2=lnscr[:, co + 2:co + 3],
                    op0=ALU.add, op1=ALU.mult),
                   waits=([xi, ("S", rstd_done), ("V", var_done), ("T", xnbf_free[0])]
                          if j == 0 else [xi]), inc="G")
            p0_state["xn_done"] = ("G", val("G"))
            p0_state["scr_done"] = val("S")

        def emit_P0_trans(evac_eng="vector"):
            xn_done = p0_state["xn_done"]
            for j in (0, 1):
                for r in range(2):
                    bank = ptb(r)
                    for ii in range(4):
                        i = r * 4 + ii
                        w = ([xn_done, CONSTS] + bank_gates(bank)) if ii == 0 else ()
                        tr(bank[:, ii * 128:(ii + 1) * 128],
                           xn_bf[:, j * 1024 + i * 128: j * 1024 + i * 128 + 128],
                           identb[:], waits=w, inc="T" if ii == 3 else None)
                    tdone = val("T")
                    dst = xnT[:].rearrange("p (i jj t) -> p i jj t", i=8, jj=2)[:, r * 4:(r + 1) * 4, j, :]
                    if evac_eng == "vector":
                        em("vector", lambda e, d=dst, b=bank: e.tensor_copy(
                            out=d, in_=b[:, 0:512].rearrange("p (a b) -> p a b", a=4)),
                           waits=[("T", tdone), ("T", kactT_free[0]),
                                  ("S", p0_state["scr_done"])], inc="V")
                        reader(bank, "V")
                    else:
                        em("scalar", lambda e, d=dst, b=bank: e.copy(
                            out=d, in_=b[:, 0:512].rearrange("p (a b) -> p a b", a=4)),
                           waits=[("T", tdone), ("T", kactT_free[0]),
                                  ("S", p0_state["scr_done"])], inc="S")
                        reader(bank, "S")
            xnbf_free[0] = val("T")
            xnT_ready[0] = (val("V"), val("S"))

        def emit_qk(wname, dest, wg, xnT_doneV, xnT_doneS):
            for i in range(8):
                bank = (pA, pB)[(i // 2) % 2]
                region = bank[:, (i % 2) * 256:(i % 2) * 256 + 256]
                for d in range(8):
                    w = []
                    if d == 0:
                        w = [("V", wg), ("V", xnT_doneV), ("S", xnT_doneS)]
                        if i % 2 == 0:
                            w += bank_gates(bank)
                    mm(region, w_sb[wname][:, d * 1024 + i * 128: d * 1024 + i * 128 + 128],
                       xnT[:, d * 256:(d + 1) * 256],
                       start=(d == 0), stop=(d == 7), waits=w, inc="T" if d == 7 else None)
                if i % 2 == 1:
                    tdone = val("T")
                    em("scalar",
                       lambda e, b=bank, i=i, dst=dest: e.activation(
                           out=dst[:, (i - 1) * 256:(i + 1) * 256], in_=b[:, 0:512],
                           func=AF.Silu),
                       waits=[("T", tdone), ("T", scT_free[0])], inc="S")
                    reader(bank, "S")

        def emit_P(sc):
            xnT_doneV, xnT_doneS = xnT_ready[0]

            if use_preload:
                em("scalar", lambda e: e.preload_act_table(AF.Silu))
            emit_qk("wq", q_sb, need_weight("wq"), xnT_doneV, xnT_doneS)
            emit_qk("wk", k_sb, need_weight("wk"), xnT_doneV, xnT_doneS)
            qk_done = val("S")

            wg = need_weight("wv")
            for jj in range(2):
                for h2 in range(2):
                    bank = (pA, pB)[(jj * 2 + h2) % 2]
                    for d in range(8):
                        w = []
                        if d == 0:
                            w = [("V", wg), ("V", xnT_doneV), ("S", xnT_doneS)] + bank_gates(bank)
                        mm(bank[:, 0:512],
                           xnT[:, d * 256 + jj * 128: d * 256 + jj * 128 + 128],
                           w_sb["wv"][:, d * 1024 + h2 * 512: d * 1024 + h2 * 512 + 512],
                           start=(d == 0), stop=(d == 7), waits=w, inc="T" if d == 7 else None)
                    tdone = val("T")
                    em("scalar",
                       lambda e, b=bank, jj=jj, h2=h2: e.copy(
                           out=v_sb[:, jj * 1024 + h2 * 512: jj * 1024 + h2 * 512 + 512],
                           in_=b[:, 0:512]),
                       waits=[("T", tdone), ("T", scT_free[0])], inc="S")
                    reader(bank, "S")

            pre = {}
            for h in range(4):
                for ii in range(2):
                    g = 2 * h + ii
                    w = ([("S", qk_done)] + bank_gates(pD)) if (h == 0 and ii == 0) else ()
                    mm(pD[:, h * 128:(h + 1) * 128],
                       k_sb[:, g * 256: g * 256 + 128],
                       q_sb[:, g * 256: g * 256 + 128],
                       start=(ii == 0), stop=(ii == 1), waits=w,
                       inc="T" if (h == 3 and ii == 1) else None)
            pre["a_done"] = val("T")

            wg = need_weight("wf")
            for jj in range(2):
                for d in range(8):
                    w = []
                    if d == 0:
                        w = [("V", wg), ("V", xnT_doneV), ("S", xnT_doneS)]
                        if jj == 0:
                            w.append(("V", pC_free[0]))
                    mm(pC[:, jj * 256:(jj + 1) * 256],
                       xnT[:, d * 256 + jj * 128: d * 256 + jj * 128 + 128],
                       wf_sb[:, d * 256:(d + 1) * 256],
                       start=(d == 0), stop=(d == 7), waits=w, inc="T" if d == 7 else None)
            f_done = val("T")

            need_weight("wo"); need_weight("wrt"); need_weight("wkt")
            allw = need_weight("wvt")
            em("vector", lambda e: e.tensor_copy(out=xnprevT[:, 1:2048], in_=xnT[:, 0:2047]),
               waits=[("V", allw), CONSTS])
            em("vector", lambda e: e.tensor_copy(
                out=xnprevT[:, 0:2048].rearrange("p (i t) -> p i t", i=8)[:, :, 0],
                in_=xtail[:, 0:8]), waits=[("V", val("V"))])
            em("vector", lambda e: e.tensor_tensor(
                out=xr_t[:, 2048:4096], in0=xnT[:, 0:2048], in1=xnprevT[:, 0:2048],
                op=ALU.subtract), waits=[("V", val("V"))])
            for (mu, off) in ((mur, 0), (muk, 2048)):
                for i in range(8):
                    b = i * 256
                    em("vector", lambda e, b=b, i=i, mu=mu, off=off: e.scalar_tensor_tensor(
                        out=xr_t[:, off + b: off + b + 256], in0=xr_t[:, 2048 + b: 2048 + b + 256],
                        scalar=mu[:, i:i + 1], in1=xnprevT[:, b:b + 256],
                        op0=ALU.mult, op1=ALU.add),
                       waits=[("V", val("V"))] if i == 0 else ())
            em("vector", lambda e: e.tensor_copy(
                out=xtail[:, 0:8],
                in_=xnT[:].rearrange("p (i t) -> p i t", i=8)[:, :, 255]),
               waits=[("V", val("V"))], inc="V")
            mix_done = val("V")

            for jj in range(2):
                for h2 in range(2):
                    bank = (pA, pB)[(jj * 2 + h2) % 2]
                    for i in range(8):
                        w = []
                        if i == 0:
                            w = [("V", mix_done)] + bank_gates(bank)
                        mm(bank[:, 0:512],
                           xr_t[:, i * 256 + jj * 128: i * 256 + jj * 128 + 128],
                           w_sb["wrt"][:, i * 1024 + h2 * 512: i * 1024 + h2 * 512 + 512],
                           start=(i == 0), stop=(i == 7), waits=w, inc="T" if i == 7 else None)
                    tdone = val("T")
                    em("scalar", lambda e, b=bank, jj=jj, h2=h2: e.activation(
                        out=t_r[:, jj * 1024 + h2 * 512: jj * 1024 + h2 * 512 + 512],
                        in_=b[:, 0:512], func=AF.Tanh, scale=0.5),
                       waits=[("T", tdone)], inc="S")
                    reader(bank, "S")

            em("vector", lambda e: e.tensor_tensor(out=Am_sb[:], in0=pD[:, 0:512],
                                                   in1=mask4[:], op=ALU.mult),
               waits=[("T", pre["a_done"]), CONSTS, ("V", val("V"))], inc="V")
            reader(pD, "V")
            pre["am_done"] = val("V")

            if use_preload:
                em("scalar", lambda e: e.preload_act_table(AF.Exp))
            em("scalar", lambda e: e.activation(out=t_e[:], in_=pC[:, 0:512],
                                                func=AF.Exp, scale=-1.0),
               waits=[("T", f_done)])
            em("scalar", lambda e: e.activation(out=t_l[:], in_=t_e[:],
                                                func=AF.Ln, bias=c_one[:, 0:1]),
               waits=[("S", val("S"))], inc="S")
            l_all = val("S")
            em("vector", lambda e: e.tensor_copy(out=l_hi[:], in_=t_l[:]),
               waits=[("S", l_all), ("T", scT_free[0])])
            em("vector", lambda e: e.tensor_tensor(out=l_lo[:], in0=t_l[:], in1=l_hi[:],
                                                   op=ALU.subtract),
               waits=[("V", val("V"))], inc="V")
            hilo_all = val("V")

            for i2 in range(8):
                bank = (pA, pB)[(i2 // 2) % 2]
                region = bank[:, (i2 % 2) * 256:(i2 % 2) * 256 + 256]
                for d in range(8):
                    w = []
                    if d == 0:
                        w = [("V", mix_done)]
                        if i2 % 2 == 0:
                            w += bank_gates(bank)
                    mm(region,
                       w_sb["wkt"][:, d * 1024 + i2 * 128: d * 1024 + i2 * 128 + 128],
                       xr_t[:, 2048 + d * 256: 2048 + (d + 1) * 256],
                       start=(d == 0), stop=(d == 7), waits=w, inc="T" if d == 7 else None)
                if i2 % 2 == 1:
                    tdone = val("T")
                    em("scalar", lambda e, b=bank: e.activation(
                        out=krelu[:], in_=b[:, 0:512], func=AF.Relu),
                       waits=[("T", tdone), ("V", krelu_free[0])], inc="S")
                    sdone = val("S")
                    reader(bank, "S")
                    em("vector", lambda e, i2=i2: e.tensor_tensor(
                        out=kact[:, (i2 - 1) * 256:(i2 + 1) * 256], in0=krelu[:], in1=krelu[:],
                        op=ALU.mult),
                       waits=[("S", sdone), ("T", f_done)], inc="V")
                    krelu_free[0] = val("V")
            kact_done = val("V")

            for jj in range(2):
                mm(pC[:, jj * 256:(jj + 1) * 256], utri[:], l_hi[:, jj * 256:(jj + 1) * 256],
                   start=True, stop=False,
                   waits=[("V", hilo_all), ("S", l_all), CONSTS] if jj == 0 else ())
                mm(pC[:, jj * 256:(jj + 1) * 256], utri[:], l_lo[:, jj * 256:(jj + 1) * 256],
                   start=False, stop=True, inc="T" if jj == 1 else None)
            d_all = val("T")

            em("scalar", lambda e: e.activation(out=lam_sb[:], in_=pC[:, 0:512], func=AF.Exp),
               waits=[("T", d_all)])
            em("scalar", lambda e: e.activation(out=t_linv[:], in_=pC[:, 0:512],
                                                func=AF.Exp, scale=-1.0))
            em("scalar", lambda e: e.activation(out=t_e[:], in_=t_l[:], func=AF.Exp,
                                                scale=-0.125), waits=[("S", val("S"))], inc="S")
            s5_done = val("S")
            em("vector", lambda e: e.tensor_scalar(
                out=t_e[:], in0=t_e[:], scalar1=-1.0, scalar2=1.0,
                op0=ALU.mult, op1=ALU.add), waits=[("S", s5_done), ("V", val("V"))])
            em("vector", lambda e: e.tensor_tensor(out=Sp_sb[:], in0=t_e[:], in1=t_linv[:],
                                                   op=ALU.mult),
               waits=[("V", val("V"))], inc="V")
            Sp_done = val("V")

            for jj in range(2):
                c = 2 * sc + jj
                for h2 in range(2):
                    bank = (pA, pB)[(jj * 2 + h2) % 2]
                    for i2 in range(8):
                        w = []
                        if i2 == 0:
                            w = [("V", kact_done)] + bank_gates(bank)
                        mm(bank[:, 0:512],
                           kact[:, i2 * 256 + jj * 128: i2 * 256 + jj * 128 + 128],
                           w_sb["wvt"][:, i2 * 1024 + h2 * 512: i2 * 1024 + h2 * 512 + 512],
                           start=(i2 == 0), stop=(i2 == 7), waits=w, inc="T" if i2 == 7 else None)
                    tdone = val("T")
                    kactT_free[0] = tdone
                    em("vector", lambda e, b=bank, jj=jj, h2=h2: e.scalar_tensor_tensor(
                        out=tmp512[:],
                        in0=t_r[:, jj * 1024 + h2 * 512: jj * 1024 + h2 * 512 + 512],
                        scalar=1.0, in1=b[:, 0:512], op0=ALU.add, op1=ALU.mult),
                       waits=[("T", tdone), ("S", val("S")), ("V", val("V"))])
                    em("vector", lambda e, jj=jj, h2=h2, c=c: e.tensor_tensor(
                        out=fstage[:, jj * 1024 + h2 * 512: jj * 1024 + h2 * 512 + 512],
                        in0=tmp512[:], in1=xb[c % 2][:, h2 * 512:(h2 + 1) * 512],
                        op=ALU.add), waits=[("V", val("V"))], inc="V")
                    reader(bank, "V")
            F_done = val("V")
            pf = [2 * sc + j + 2 for j in (0, 1) if 2 * sc + j + 2 < NCH]
            for c in pf:
                xin_val[c] = xdma(c, "gpsimd", waits=[("V", F_done)])
            for c in pf:
                xin_val[c] = xin_val[pf[-1]]



            return Sp_done, s5_done, pre

        def emit_R_ha(sc, j, Sp_done, s5_done, pre):
            c = 2 * sc + j
            first = (c == 0)
            par = c % 2
            am_done = pre["am_done"]

            for h in range(4):
                w = []
                if h == 0:
                    w = [("V", am_done), ("V", Sp_done), ("S", s5_done),
                         ("V", pC_free[0])]
                mm(pC[:, j * 256 + h * 64: j * 256 + h * 64 + 64],
                   Am_sb[:, h * 128:(h + 1) * 128],
                   Sp_sb[:, j * 256 + h * 64: j * 256 + h * 64 + 64],
                   start=True, stop=first, waits=w,
                   inc=("T" if (first and h == 3) else None))
                if not first:
                    for ii in range(2):
                        wi = ([("V", hkbf_done[0])] if (h == 0 and ii == 0) else ())
                        mm(pC[:, j * 256 + h * 64: j * 256 + h * 64 + 64],
                           q_sb[:, (2 * h + ii) * 256 + j * 128:
                                (2 * h + ii) * 256 + j * 128 + 128],
                           hk_bf[:, (ii * 4 + h) * 64: (ii * 4 + h) * 64 + 64],
                           start=False, stop=(ii == 1), waits=wi,
                           inc=("T" if (h == 3 and ii == 1) else None))
            lg_done = val("T")

            base = 32 * par
            sm_lg = t_e[:, 0:256]
            sm_e = t_l[:, 0:256]
            nm4 = t_linv[:, base + 0:base + 4]
            nmS4 = t_linv[:, base + 4:base + 8]
            ssm4 = t_linv[:, base + 8:base + 12]
            rs4 = t_linv[:, base + 12:base + 16]
            v2_ = t_linv[:, base + 16:base + 20]
            v3_ = t_linv[:, base + 20:base + 24]
            sc4 = t_linv[:, base + 24:base + 28]
            em("vector", lambda e: e.tensor_tensor(
                out=sm_lg, in0=pC[:, j * 256:(j + 1) * 256],
                in1=lam_sb[:, j * 256:(j + 1) * 256], op=ALU.mult),
               waits=[("T", lg_done), ("V", val("V"))])
            reader(pC, "V")
            em("vector", lambda e: e.reduce_max(
                out=nm4[:, 0:1], in_=sm_lg, axis=AX, negate=True),
               waits=[("V", val("V"))])
            em("vector", lambda e: e.tensor_scalar_mul(out=nmS4[:, 0:1], in0=nm4[:, 0:1],
                                                       scalar1=SCALE),
               waits=[("V", val("V"))], inc="V")
            nms_done = val("V")

            bank = ptb(0)
            for h in range(4):
                w = bank_gates(bank) if h == 0 else ()
                tr(bank[0:64, h * 128:(h + 1) * 128],
                   Sp_sb[:, j * 256 + h * 64: j * 256 + h * 64 + 64], identb[:],
                   waits=w, inc="T" if h == 3 else None)
            tdone = val("T")
            em("vector", lambda e, b=bank: e.tensor_copy(out=ST_sb[0:64, 0:512],
                                                         in_=b[0:64, 0:512]),
               waits=[("T", tdone)], inc="V")
            reader(bank, "V")

            em("scalar", lambda e: e.activation(
                out=sm_e, in_=sm_lg, func=AF.Exp, scale=SCALE, bias=nmS4[:, 0:1]),
               waits=[("V", nms_done)], inc="S")
            sme_done = val("S")

            for half in range(2):
                bank = ptb(1 - half)
                for ii in range(4):
                    i = half * 4 + ii
                    w = bank_gates(bank) if ii == 0 else ()
                    tr(bank[:, ii * 128:(ii + 1) * 128],
                       k_sb[:, i * 256 + j * 128: i * 256 + j * 128 + 128], identb[:],
                       waits=w, inc="T" if ii == 3 else None)
                tdone = val("T")
                em("scalar", lambda e, half=half, b=bank: e.copy(
                    out=kT_sb[:, half * 512:(half + 1) * 512], in_=b[:, 0:512]),
                   waits=[("T", tdone), ("T", scT_free[0])], inc="S")
                reader(bank, "S")
            kt_done = val("S")
            em("vector", lambda e: e.reduce_sum(
                out=ssm4, in_=sm_e.rearrange("p (h m) -> p h m", h=4), axis=AX),
               waits=[("S", sme_done)])
            em("vector", lambda e: e.reciprocal(out=rs4, in_=ssm4), waits=[("V", val("V"))])
            em("vector", lambda e: e.tensor_tensor(
                out=Pt_sb[:], in0=sm_e, in1=lam_sb[:, j * 256:(j + 1) * 256], op=ALU.mult),
               waits=[("V", val("V"))], inc="V")
            pt_done = val("V")

            bank = ptb(1)
            for h in range(4):
                w = (bank_gates(bank) + [("V", pt_done)]) if h == 0 else ()
                tr(bank[0:64, h * 128:(h + 1) * 128], Pt_sb[:, h * 64:(h + 1) * 64], identb[:],
                   waits=w, inc="T" if h == 3 else None)
            tdone = val("T")
            em("vector", lambda e, b=bank: e.tensor_copy(out=PtT_sb[0:64, 0:512],
                                                         in_=b[0:64, 0:512]),
               waits=[("T", tdone)], inc="V")
            reader(bank, "V")
            ptT_done = val("V")

            for h in range(4):
                w = ([("V", ptT_done)] + bank_gates(pB)) if h == 0 else ()
                mm(pB[:, h * 128:(h + 1) * 128],
                   ST_sb[0:64, h * 128:(h + 1) * 128],
                   PtT_sb[0:64, h * 128:(h + 1) * 128],
                   start=True, stop=True, waits=w, inc="T" if h == 3 else None)
            b_done = val("T")
            em("vector", lambda e: e.tensor_tensor(out=Bm_sb[:], in0=pB[:, 0:512],
                                                   in1=mask4[:], op=ALU.mult),
               waits=[("T", b_done)], inc="V")
            reader(pB, "V")
            return dict(c=c, first=first, par=par, j=j, bm_done=val("V"),
                        pt_done=pt_done, kt_done=kt_done,
                        rs4=rs4, v2_=v2_, v3_=v3_, sc4=sc4)

        def emit_R_hb(sc, j, st):
            c, first, par = st["c"], st["first"], st["par"]
            bm_done, pt_done, kt_done = st["bm_done"], st["pt_done"], st["kt_done"]

            for h in range(4):
                bank = pD if h < 2 else pE
                region = bank[:, (h % 2) * 256:(h % 2) * 256 + 256]
                w = ([("V", bm_done), ("V", hvbf_done[0])] + bank_gates(bank)) \
                    if h % 2 == 0 else ()
                last = (h % 2 == 1)
                mm(region, Bm_sb[:, h * 128:(h + 1) * 128],
                   v_sb[:, j * 1024 + h * 256: j * 1024 + h * 256 + 256],
                   start=True, stop=first, waits=w,
                   inc=("T" if (first and last) else None))
                if not first:
                    mm(region, PtT_sb[0:64, h * 128:(h + 1) * 128],
                       hv_bf[0:64, h * 256:(h + 1) * 256],
                       start=False, stop=True, inc=("T" if last else None))
                if h == 1:
                    st["oD_done"] = val("T")
            st["o_done"] = val("T")

            mm(pB[:, 0:256], sellast[:], lam_sb[:, j * 256:(j + 1) * 256],
               start=True, stop=True, waits=[("V", pt_done), CONSTS] + bank_gates(pB),
               inc="T")
            lbc_done = val("T")
            em("vector", lambda e: e.tensor_copy(out=lbc_sb[:], in_=pB[:, 0:256]),
               waits=[("T", lbc_done)])
            em("vector", lambda e: e.tensor_tensor(out=spp[:],
                                                   in0=Sp_sb[:, j * 256:(j + 1) * 256],
                                                   in1=lbc_sb[:], op=ALU.mult),
               waits=[("V", val("V"))], inc="V")
            spp_done = val("V")
            reader(pB, "V")
            letc_done = 0
            if not first:
                for h in range(4):
                    w = [("V", spp_done)] if h == 0 else ()
                    mm(pB[0:64, 256 + h: 256 + h + 1],
                       lbc_sb[:, h * 64:(h + 1) * 64], identf[:, 0:1],
                       start=True, stop=True, waits=w, inc="T" if h == 3 else None)
                tdone = val("T")
                em("vector", lambda e: e.tensor_copy(
                    out=letc[0:64, 0:4], in_=pB[0:64, 256:260]),
                   waits=[("T", tdone)], inc="V")
                reader(pB, "V")
                letc_done = val("V")
            st["spp_done"] = spp_done
            st["letc_done"] = letc_done

            for h in range(4):
                for ii in range(2):
                    w = ([("V", spp_done), ("S", kt_done)] + bank_gates(pF))                         if (h == 0 and ii == 0) else ()
                    mm(pF[:, (ii * 4 + h) * 64: (ii * 4 + h) * 64 + 64],
                       kT_sb[:, (2 * h + ii) * 128: (2 * h + ii) * 128 + 128],
                       spp[:, h * 64:(h + 1) * 64],
                       start=True, stop=True, waits=w,
                       inc="T" if (h == 3 and ii == 1) else None)
            hk_done = val("T")
            st["hk_done"] = hk_done
            if first:
                em("vector", lambda e: e.tensor_copy(out=hk_sb[:], in_=pF[:, 0:512]),
                   waits=[("T", hk_done), ("V", val("V"))])
            else:
                for ii in range(2):
                    em("vector", lambda e, ii=ii: e.tensor_tensor(
                        out=hk_tmp[:, ii * 256:(ii + 1) * 256],
                        in0=hk_sb[:, ii * 256:(ii + 1) * 256], in1=lbc_sb[:], op=ALU.mult),
                       waits=[("T", hk_done), ("V", val("V"))] if ii == 0 else ())
                em("vector", lambda e: e.tensor_tensor(out=hk_sb[:], in0=hk_tmp[:],
                                                       in1=pF[:, 0:512], op=ALU.add),
                   waits=[("V", val("V"))])
            em("vector", lambda e: e.tensor_copy(out=hk_bf[:], in_=hk_sb[:]),
               waits=[("V", val("V"))], inc="V")
            hkbf_done[0] = val("V")
            reader(pF, "V")

            return st

        def emit_R_tail_a(sc, j, st):
            c, first, par = st["c"], st["first"], st["par"]
            oD_done, o_done = st["oD_done"], st["o_done"]
            spp_done, letc_done = st["spp_done"], st["letc_done"]
            rs4, v2_, v3_, sc4 = st["rs4"], st["v2_"], st["v3_"], st["sc4"]

            ob = (o_bf, kT_sb)[par]
            w0 = [("T", oD_done), ("T", obf_free[par])]
            if par == 1:
                w0.append(("T", st["hk_done"]))
            em("scalar", lambda e, ob=ob: e.copy(out=ob[:, 0:512], in_=pD[:, 0:512]),
               waits=w0, inc="S")
            reader(pD, "S")
            em("scalar", lambda e, ob=ob: e.copy(out=ob[:, 512:1024], in_=pE[:, 0:512]),
               waits=[("T", o_done)], inc="S")
            reader(pE, "S")
            ocp_done = val("S")
            em("vector", lambda e: e.tensor_tensor(out=v2_, in0=rs4, in1=rs4, op=ALU.mult),
               waits=[("V", val("V"))])

            for h in range(4):
                bank = pD if h < 2 else pE
                region = bank[0:64, (h % 2) * 256:(h % 2) * 256 + 256]
                w = ([("V", spp_done)] + bank_gates(bank)) if h % 2 == 0 else ()
                mm(region, spp[:, h * 64:(h + 1) * 64],
                   v_sb[:, j * 1024 + h * 256: j * 1024 + h * 256 + 256],
                   start=True, stop=True, waits=w, inc="T" if h == 3 else None)
            hv_done = val("T")
            hv_new, hv_old = hv_sb[par], hv_sb[1 - par]
            for h in range(4):
                bank = pD if h < 2 else pE
                region = bank[0:64, (h % 2) * 256:(h % 2) * 256 + 256]
                if first:
                    em("vector", lambda e, r=region, h=h: e.tensor_copy(
                        out=hv_new[0:64, h * 256:(h + 1) * 256], in_=r),
                       waits=[("T", hv_done), ("V", val("V"))] if h == 0 else ())
                else:
                    w = [("T", hv_done), ("V", letc_done), ("V", val("V"))] if h == 0 else ()
                    em("vector", lambda e, r=region, h=h: e.scalar_tensor_tensor(
                        out=hv_new[0:64, h * 256:(h + 1) * 256],
                        in0=hv_old[0:64, h * 256:(h + 1) * 256],
                        scalar=letc[0:64, h:h + 1], in1=r, op0=ALU.mult, op1=ALU.add),
                       waits=w)
            em("vector", lambda e: e.tensor_copy(out=hv_bf[0:64, :], in_=hv_new[0:64, :]),
               waits=[("V", val("V"))], inc="V")
            hvbf_done[0] = val("V")
            reader(pD, "V")
            reader(pE, "V")

            sq_hist = {}
            for h in range(4):
                w = [("S", ocp_done), ("V", v3_prev[0])] if h == 0 else (
                    [("S", sq_hist[h - 2])] if h >= 2 else ())
                em("scalar", lambda e, h=h, ob=ob: e.activation(
                    out=sqs[:, (h % 2) * 256:(h % 2) * 256 + 256],
                    in_=ob[:, h * 256:(h + 1) * 256], func=AF.Square,
                    accum_out=ssq[:, h:h + 1]),
                   waits=w, inc="S" if h == 3 else None)
                sq_hist[h] = val("S")
            sq_done = val("S")
            em("vector", lambda e: e.tensor_tensor(out=v3_, in0=ssq[:], in1=v2_, op=ALU.mult),
               waits=[("S", sq_done), ("V", val("V"))], inc="V")
            v3_done = val("V")
            v3_prev[0] = v3_done
            em("scalar", lambda e: e.activation(out=rstd4[:], in_=v3_, func=AF.Ln,
                                                scale=1.0 / 256.0, bias=c_eps[:, 0:1]),
               waits=[("V", v3_done)])
            em("scalar", lambda e: e.activation(out=rstd4[:], in_=rstd4[:], func=AF.Exp,
                                                scale=-0.5), waits=[("S", val("S"))], inc="S")
            rstd_done = val("S")
            em("vector", lambda e: e.tensor_tensor(out=sc4, in0=rstd4[:], in1=rs4,
                                                   op=ALU.mult),
               waits=[("S", rstd_done), ("V", val("V"))], inc="V")
            obf_pair = []
            for h in range(4):
                w = [("V", val("V"))] if h == 0 else ()
                em("vector", lambda e, h=h, ob=ob: e.tensor_scalar_mul(
                    out=ob[:, h * 256:(h + 1) * 256],
                    in0=ob[:, h * 256:(h + 1) * 256], scalar1=sc4[:, h:h + 1]),
                   waits=w, inc="V" if h % 2 == 1 else None)
                if h % 2 == 1:
                    obf_pair.append(val("V"))
            st["obf_pair"] = obf_pair
            return st

        def emit_R_tail_b(sc, j, st):
            c, first, par = st["c"], st["first"], st["par"]
            obf_pair = st["obf_pair"]

            ob = (o_bf, kT_sb)[par]
            for r in range(2):
                bank = ptb(r)
                for ii in range(4):
                    i = r * 4 + ii
                    w = (bank_gates(bank) + [("V", obf_pair[r])]) if ii == 0 else ()
                    tr(bank[:, ii * 128:(ii + 1) * 128], ob[:, i * 128:(i + 1) * 128],
                       identb[:], waits=w, inc="T" if ii == 3 else None)
                tdone = val("T")
                em("scalar", lambda e, r=r, b=bank, j=j: e.copy(
                    out=oT_sb[:, j * 1024 + r * 512: j * 1024 + (r + 1) * 512],
                    in_=b[:, 0:512]),
                   waits=[("T", tdone)], inc="S")
                reader(bank, "S")
            oT_done = val("S")
            obf_free[par] = val("T")

            for h2 in range(2):
                bank = (pC, pF)[h2]
                for i in range(8):
                    w = ([("S", oT_done)] + bank_gates(bank)) if i == 0 else ()
                    mm(bank[:, 0:512], oT_sb[:, j * 1024 + i * 128: j * 1024 + (i + 1) * 128],
                       w_sb["wo"][:, i * 1024 + h2 * 512: i * 1024 + h2 * 512 + 512],
                       start=(i == 0), stop=(i == 7), waits=w, inc="T" if i == 7 else None)
                tdone = val("T")
                ow = [("T", tdone)]
                if out_n[0] >= 2:
                    ow.append(("OUT%d" % (out_n[0] % 2), 16 * ((out_n[0] - 2) // 2 + 1)))
                em("vector", lambda e, b=bank, h2=h2, j=j, par=par: e.tensor_tensor(
                    out=out_sb[par][:, h2 * 512:(h2 + 1) * 512],
                    in0=b[:, 0:512],
                    in1=fstage[:, j * 1024 + h2 * 512: j * 1024 + h2 * 512 + 512],
                    op=ALU.add), waits=ow + [("V", val("V"))], inc="V")
                reader(bank, "V")
            comb_done = val("V")
            pC_free[0] = comb_done
            xnbf_free[0] = val("T")
            out_n[0] += 1
            em("gpsimd", lambda e, c=c, par=par: e.dma_start(
                out=out_d[c * 128:(c + 1) * 128, :], in_=out_sb[par][:]),
               waits=[("V", comb_done)], inc="OUT%d" % (c % 2), inc_by=16)

        def emit_A(j):
            pre = {}
            for h in range(4):
                for ii in range(2):
                    g = 2 * h + ii
                    w = bank_gates(pA) if (h == 0 and ii == 0) else ()
                    mm(pA[:, h * 128:(h + 1) * 128],
                       k_sb[:, g * 256 + j * 128: g * 256 + j * 128 + 128],
                       q_sb[:, g * 256 + j * 128: g * 256 + j * 128 + 128],
                       start=(ii == 0), stop=(ii == 1), waits=w,
                       inc="T" if (h == 3 and ii == 1) else None)
            pre["a_done"] = val("T")
            em("vector", lambda e: e.tensor_tensor(out=Am_sb[:], in0=pA[:, 0:512],
                                                   in1=mask4[:], op=ALU.mult),
               waits=[("T", pre["a_done"]), CONSTS], inc="V")
            reader(pA, "V")
            pre["am_done"] = val("V")
            return pre

        emit_P0_stats(0)
        emit_P0_trans("vector")
        for sc in range(NSC):
            Sp_done, s5_done, pre = emit_P(sc)
            st0 = emit_R_ha(sc, 0, Sp_done, s5_done, pre)
            pre1 = emit_A(1)
            st0 = emit_R_hb(sc, 0, st0)
            st1 = emit_R_ha(sc, 1, Sp_done, s5_done, pre1)
            st0 = emit_R_tail_a(sc, 0, st0)
            st1 = emit_R_hb(sc, 1, st1)
            emit_R_tail_b(sc, 0, st0)
            if sc + 1 < NSC:
                emit_P0_stats(sc + 1)
            st1 = emit_R_tail_a(sc, 1, st1)
            if sc + 1 < NSC:
                emit_P0_trans("vector")
            emit_R_tail_b(sc, 1, st1)
            scT_free[0] = val("T")

        em("gpsimd", lambda e: e.nop(),
           waits=[("OUT0", 16 * ((NCH + 1) // 2)), ("OUT1", 16 * (NCH // 2))])

        @blk.sync
        def _(e):
            sch.replay("sync", e, sems)

        @blk.gpsimd
        def _(e):
            sch.replay("gpsimd", e, sems)

        @blk.tensor
        def _(e):
            sch.replay("tensor", e, sems)

        @blk.vector
        def _(e):
            sch.replay("vector", e, sems)

        @blk.scalar
        def _(e):
            sch.replay("scalar", e, sems)

    return nc


def _prep_inputs(inputs):
    f32 = np.float32
    x = np.ascontiguousarray(np.asarray(inputs["L0_attn_x"], dtype=f32))
    com = {
        "wq": np.ascontiguousarray(np.asarray(inputs["L0_attn_Wq"], f32)),
        "wk": np.ascontiguousarray(np.asarray(inputs["L0_attn_Wk"], f32)),
        "wv": np.ascontiguousarray(np.asarray(inputs["L0_attn_Wv"], f32)),
        "wf": np.ascontiguousarray(np.asarray(inputs["L0_attn_Wf"], f32)),
        "wo": np.ascontiguousarray(np.asarray(inputs["L0_attn_Wo"], f32)),
        "wrt": np.ascontiguousarray(np.asarray(inputs["L0_ffn_W_r"], f32).T),
        "wkt": np.ascontiguousarray(np.asarray(inputs["L0_ffn_W_k"], f32).T),
        "wvt": np.ascontiguousarray(np.asarray(inputs["L0_ffn_W_v"], f32).T),
        "mur": np.ascontiguousarray(np.asarray(inputs["L0_ffn_mu_r"], f32).reshape(8, 128).T),
        "muk": np.ascontiguousarray(np.asarray(inputs["L0_ffn_mu_k"], f32).reshape(8, 128).T),
        "gnw": np.ascontiguousarray(np.asarray(inputs["L0_attn_g_norm_w"], f32).reshape(2, 128).T),
        "identb": np.eye(128, dtype=ml_dtypes.bfloat16),
        "identf": np.eye(128, dtype=np.float32),
        "mask4": np.ascontiguousarray(
            np.tile(np.triu(np.ones((128, 128), np.float32)), (1, 4)).astype(ml_dtypes.bfloat16)),
        "utri": np.ascontiguousarray(
            (np.triu(np.ones((128, 128), np.float32)) * -0.125).astype(ml_dtypes.bfloat16)),
        "sellast": np.ascontiguousarray(
            ((np.arange(128)[:, None] == 127) * np.ones((1, 128))).astype(np.float32)),
    }
    return x, com


def kernel(**inputs):
    x, com = _prep_inputs(inputs)
    nc = build_program(T=4096)
    in_maps = []
    for b in range(8):
        m = {"x": np.ascontiguousarray(x[b])}
        m.update(com)
        in_maps.append(m)
    res = run_bass_kernel_spmd(nc, in_maps, core_ids=list(range(8)))
    out = np.stack([np.asarray(r["out"]) for r in res.results], axis=0)
    return out.astype(np.float32)
